# version 06
# baseline (speedup 1.0000x reference)
import contextlib
import numpy as np
import ml_dtypes
import concourse.bass as bass
import concourse.mybir as mybir
from concourse.bass_utils import run_bass_kernel_spmd

F32 = mybir.dt.float32
BF16 = mybir.dt.bfloat16
I32 = mybir.dt.int32
AF = mybir.ActivationFunctionType
OP = mybir.AluOpType
AX = mybir.AxisListType

MAGIC = 12582912.0
EPS = 1e-5
NCORES = 8
B_PER_CORE = 2
L = 4096
D = 1024
FF = 4096
H = 16
CH = 64
SEG = 512
NSEG = L // SEG
NCH = SEG // CH
NTT = SEG // 128

PAIRS = [(i, j) for i in range(16) for j in range(i, 16)]
TA, TB = PAIRS[:68], PAIRS[68:]
NTILE = 2

_DEBUG = False
_DBG_SKIP = frozenset()

CONSTS = [MAGIC, MAGIC + 1024.0, MAGIC - 1024.0, -4.0, -3.75, 256.0,
          1.0 / 64.0, 1.0 / 256.0, 0.25, -49152.0, -49156.0, EPS,
          -1024.5, -(MAGIC - 1024.0)]


def _bf(x):
    return np.asarray(x, dtype=np.float32).astype(ml_dtypes.bfloat16)


def host_consts():
    d = {}
    d["cpool"] = np.tile(np.array(CONSTS, np.float32), (128, 1))
    rs = np.zeros((128, 8 * NTILE * 3 * 68), np.float32)
    for h8 in range(8):
        for t in range(NTILE):
            pairs = (TA, TB)[t]
            for kind in range(3):
                base = ((h8 * NTILE + t) * 3 + kind) * 68
                for pp, (i, j) in enumerate(pairs):
                    if kind == 0:
                        rs[h8 * 16 + i, base + pp] = 1.0
                    elif kind == 1:
                        rs[h8 * 16 + j, base + pp] = 1.0
                    else:
                        rs[h8 * 16 + j, base + pp] = 2.0 if i != j else 1.0
    d["rs"] = _bf(rs)
    rm = np.zeros((128, 8 * 2 * 136), np.float32)
    for h8 in range(8):
        for kind in range(2):
            base = (h8 * 2 + kind) * 136
            for p, (i, j) in enumerate(PAIRS):
                if kind == 0:
                    rm[h8 * 16 + i, base + p] = 1.0
                else:
                    rm[h8 * 16 + j, base + p] = 2.0 if i != j else 1.0
    d["rm"] = _bf(rm)
    m = np.zeros((64, H * 64), np.float32)
    tril = (np.arange(64)[:, None] <= np.arange(64)[None, :]).astype(np.float32)
    for h in range(H):
        m[:, h * 64:(h + 1) * 64] = tril
    d["maskrep"] = _bf(m)
    d["ident"] = _bf(np.eye(128, dtype=np.float32))
    return d


class Rec:
    ENG = "TVSQG"

    def __init__(self):
        self.ops = {e: [] for e in self.ENG}
        self.cnt = {}
        self.waited = {}
        self.streams = set()

    def emit(self, e, fn):
        self.ops[e].append(("i", [fn, 0, e]))
        return self

    def t(self, fn):
        return self.emit("T", fn)

    def v(self, fn):
        return self.emit("V", fn)

    def s(self, fn):
        return self.emit("S", fn)

    def q(self, fn, stream):
        self.streams.add(stream)
        self.ops["Q"].append(("i", [fn, 16, stream]))
        self.cnt[stream] = self.cnt.get(stream, 0) + 16
        return (stream, self.cnt[stream])

    def g(self, fn, stream):
        self.streams.add(stream)
        self.ops["G"].append(("i", [fn, 16, stream]))
        self.cnt[stream] = self.cnt.get(stream, 0) + 16
        return (stream, self.cnt[stream])

    def tick(self, e):
        for kind, ent in reversed(self.ops[e]):
            if kind == "i":
                if ent[1] == 0:
                    ent[1] = 1
                    self.cnt[e] = self.cnt.get(e, 0) + 1
                return (e, self.cnt[e])
        raise RuntimeError(f"no instruction on {e}")

    def final_waits(self, e):
        for st in sorted(self.streams):
            self.ops[e].append(("w", (st, self.cnt.get(st, 0))))

    def wait(self, e, *evs):
        for ev in evs:
            if ev is None:
                continue
            pe, v = ev
            if pe == e:
                continue
            k = (e, pe)
            if self.waited.get(k, 0) >= v:
                continue
            self.waited[k] = v
            self.ops[e].append(("w", (pe, v)))

    def replay(self, nc, sems):
        def run_engine(e, eng):
            for kind, payload in self.ops[e]:
                if kind == "w":
                    pe, v = payload
                    eng.wait_ge(sems[pe], v)
                else:
                    fn, inc, sem = payload
                    ins = fn(eng)
                    if inc:
                        ins.then_inc(sems[sem], inc)

        with nc.Block() as block:
            @block.tensor
            def _(eng):
                run_engine("T", eng)

            @block.vector
            def _(eng):
                run_engine("V", eng)

            @block.scalar
            def _(eng):
                run_engine("S", eng)

            @block.sync
            def _(eng):
                run_engine("Q", eng)

            @block.gpsimd
            def _(eng):
                run_engine("G", eng)


def ap(t, poff, coff, dims):
    rw = t.shape[1]
    return bass.AP(t, poff * rw + coff, [[rw, dims[0][1]]] + list(dims[1:]))


def build_nc(dbg=False):
    nc = bass.Bass("TRN2", target_bir_lowering=False,
                   detect_race_conditions=False)
    x_d = nc.dram_tensor("x", [B_PER_CORE * L, D], F32, kind="ExternalInput").ap()
    wq_d = nc.dram_tensor("wq", [D, 256], F32, kind="ExternalInput").ap()
    wk_d = nc.dram_tensor("wk", [D, 256], F32, kind="ExternalInput").ap()
    wv_d = nc.dram_tensor("wv", [D, D], F32, kind="ExternalInput").ap()
    wo_d = nc.dram_tensor("wo", [D, D], F32, kind="ExternalInput").ap()
    w1_d = nc.dram_tensor("w1", [D, FF], F32, kind="ExternalInput").ap()
    b1_d = nc.dram_tensor("b1", [FF], F32, kind="ExternalInput").ap()
    w2_d = nc.dram_tensor("w2", [FF, D], F32, kind="ExternalInput").ap()
    b2_d = nc.dram_tensor("b2", [D], F32, kind="ExternalInput").ap()
    cp_d = nc.dram_tensor("cpool", [128, len(CONSTS)], F32, kind="ExternalInput").ap()
    rs_d = nc.dram_tensor("rs", [128, 8 * NTILE * 3 * 68], BF16, kind="ExternalInput").ap()
    rm_d = nc.dram_tensor("rm", [128, 8 * 2 * 136], BF16, kind="ExternalInput").ap()
    mk_d = nc.dram_tensor("maskrep", [64, H * 64], BF16, kind="ExternalInput").ap()
    id_d = nc.dram_tensor("ident", [128, 128], BF16, kind="ExternalInput").ap()
    out_d = nc.dram_tensor("out", [B_PER_CORE * L, D], F32, kind="ExternalOutput").ap()
    dbg_d = {}
    if dbg:
        for name, shape in [("d_xn", [128, D]), ("d_q", [128, 1024]),
                            ("d_k", [128, 1024]), ("d_v1", [64, 8 * 1040]),
                            ("d_phiqa", [68, 16 * 256]), ("d_phikfa", [68, 16 * 256]),
                            ("d_phiktm", [64, 16 * 136]), ("d_otm", [64, 8 * 1024]),
                            ("d_h", [128, 512]), ("d_sz", [68, 2080]),
                            ("d_dd", [64, 16]), ("d_as", [64, 1024]),
                            ("d_qkh", [16, 8192]), ("d_po", [64, 512]), ("d_ddr", [64, 16]),
                            ("d_szc", [68, 130]), ("d_poc1", [64, 130]), ("d_ddc1", [64, 16]),
                            ("d_ln0", [128, 12])]:
            dbg_d[name] = nc.dram_tensor(name, shape, F32, kind="ExternalOutput").ap()

    es = contextlib.ExitStack()
    sb = lambda name, shape, dt=F32: es.enter_context(nc.sbuf_tensor(name, shape, dt))

    cpool = sb("cpool_sb", [128, len(CONSTS)])
    wq_sb = sb("wq_sb", [128, 2048], BF16)
    wk_sb = sb("wk_sb", [128, 2048], BF16)
    w1_sb = [sb(f"w1_sb{i}", [128, 1024], BF16) for i in range(2)]
    w2_sb = [sb(f"w2_sb{i}", [128, 1024], BF16) for i in range(2)]
    b1s = sb("b1s", [128, 32])
    b2t = sb("b2t", [128, D], BF16)
    rs_sb = sb("rs_sb", [128, 8 * NTILE * 3 * 68], BF16)
    rm_sb = sb("rm_sb", [128, 8 * 2 * 136], BF16)
    maskrep = sb("maskrep_sb", [64, H * 64], BF16)
    ident = sb("ident_sb", [128, 128], BF16)
    xnt = sb("xnt", [128, D], BF16)
    stats = sb("stats", [128, 12])
    mv = sb("mv", [128, 2])
    negm = sb("negm", [128, 1])
    rstd = sb("rstd", [128, 1])
    osb = [sb(f"osb{i}", [128, 512]) for i in range(2)]
    xn_fm = sb("xn_fm", [128, 8 * SEG], BF16)
    qk_fm = sb("qk_fm", [128, 2 * 1024], BF16)
    pq = sb("pq", [128, 16384], BF16)
    qkh = sb("qkh", [16, 8192], BF16)
    tmst = sb("tmst", [128, 256], BF16)
    phiKtm_t = sb("phiKtm", [64, H * 136], BF16)
    phiKtm = [phiKtm_t, phiKtm_t]
    pk1 = sb("pk1", [64, H * 136], BF16)
    pk2 = sb("pk2", [64, H * 136], BF16)
    As_sb = sb("As_sb", [64, H * 64], BF16)
    Ssb = sb("Ssb", [68, NTILE * H * 65])
    SZbf = sb("SZbf", [68, NTILE * H * 65], BF16)
    big = sb("big", [128, 33024], BF16)
    dd = sb("dd", [64, 16])
    ddr = sb("ddr", [64, 16])
    ep = sb("ep", [128, 6 * 512])
    epi = sb("epi", [128, 512], I32)
    u4 = None
    q3v = None

    PS = es.enter_context(nc.psum_tensor("PS", [128, 4096], F32))

    H_COLS = 0
    OTM = 16384
    V1C = 24576

    def bigv(poff, coff, dims):
        return ap(big, poff, coff, dims)

    def o_fm_v(c0, w):
        return ap(big, 0, V1C + c0, [[0, 128], [1, w]])

    def psv(bank, poff, coff, dims):
        return ap(PS, poff, bank * 512 + coff, dims)

    def epv(i, rows=128, w=512):
        return ap(ep, 0, i * 512, [[0, rows], [1, w]])

    xr_v = lambda: ap(ep, 0, 1024, [[0, 128], [1, 1024]])
    xt_v = xr_v
    xr_h = lambda dh: ap(ep, 0, 1024 + dh * 512, [[0, 128], [1, 512]])
    u4_v = lambda: epv(0)
    q3_v = lambda: epv(1)

    for i, val in enumerate(CONSTS):
        nc.const_aps.register(F32, val, cpool[:, i:i + 1])

    r = Rec()

    r.q(lambda e: e.dma_start(cpool[:], cp_d), "qset")
    r.q(lambda e: e.dma_start(rs_sb[:], rs_d), "qset")
    r.q(lambda e: e.dma_start(rm_sb[:], rm_d), "qset")
    r.q(lambda e: e.dma_start(maskrep[:], mk_d), "qset")
    r.q(lambda e: e.dma_start(ident[:], id_d), "qset")
    r.q(lambda e: e.dma_start(ap(ep, 0, 0, [[0, 128], [1, 32]]),
                              bass.AP(b1_d.tensor, 0, [[1, 128], [128, 32]]),
                              allow_slow_non_contiguous=True), "qset")
    ev_setup_q = r.q(lambda e: e.dma_start(ap(ep, 0, 2048, [[0, 1], [1, D]]),
                                           bass.AP(b2_d.tensor, 0, [[D, 1], [1, D]])), "qset")
    r.wait("V", ev_setup_q)
    r.v(lambda e: e.tensor_scalar(b1s[:], ap(ep, 0, 0, [[0, 128], [1, 32]]),
                                  256.0, None, OP.mult))
    ev_b1s = r.tick("V")
    r.v(lambda e: e.tensor_scalar(ap(big, 0, 0, [[0, 1], [1, D]]),
                                  ap(ep, 0, 2048, [[0, 1], [1, D]]),
                                  256.0, None, OP.mult))
    r.v(lambda e: e.memset(ap(big, 0, 1024, [[0, 1], [1, 128]]), 1.0))
    ev_b2c = r.tick("V")
    r.wait("T", ev_b2c)
    for jj in range(2):
        r.t(lambda e, jj=jj: e.matmul(
            psv(jj, 0, 0, [[0, 128], [1, 512]]),
            ap(big, 0, 1024, [[0, 1], [1, 128]]),
            ap(big, 0, jj * 512, [[0, 1], [1, 512]]),
            start=True, stop=True))
    ev_b2m = r.tick("T")
    r.wait("V", ev_b2m)
    for jj in range(2):
        r.v(lambda e, jj=jj: e.tensor_copy(b2t[:, jj * 512:(jj + 1) * 512],
                                           psv(jj, 0, 0, [[0, 128], [1, 512]])))
    ev_b2t = r.tick("V")
    ev_wq = r.g(lambda e: e.dma_start(
        wq_sb[:], bass.AP(wq_d.tensor, 0, [[256, 128], [128 * 256, 8], [1, 256]])), "gwq")
    ev_wk = r.g(lambda e: e.dma_start(
        wk_sb[:], bass.AP(wk_d.tensor, 0, [[256, 128], [128 * 256, 8], [1, 256]])), "gwk")
    ev_id = ev_mk = ev_rs = ev_setup_q
    setup_evs = [ev_setup_q, ev_b1s, ev_b2t, ev_wq, ev_wk]

    def load_w1(buf, fft):
        return r.g(lambda e: e.dma_start(
            w1_sb[buf][:], bass.AP(w1_d.tensor, fft * 128,
                                   [[FF, 128], [128 * FF, 8], [1, 128]])), f"gw1{buf}")

    def load_w2(buf, dh, sl):
        return r.g(lambda e: e.dma_start(
            w2_sb[buf][:], bass.AP(w2_d.tensor, (sl * 256) * D + dh * 512,
                                   [[D, 128], [128 * D, 2], [1, 512]])), f"gw2{buf}")

    def load_wo(dh):
        return r.g(lambda e: e.dma_start(
            ap(pq, 0, 8192, [[0, 128], [512, 8], [1, 512]]),
            bass.AP(wo_d.tensor, dh * 512, [[D, 128], [128 * D, 8], [1, 512]])), "gwo")

    def load_wv():
        return r.g(lambda e: e.dma_start(
            ap(pq, 0, 8192, [[0, 128], [1024, 8], [1, 1024]]),
            bass.AP(wv_d.tensor, 0, [[D, 128], [128 * D, 8], [1, D]])), "gwv")

    dbg_done = set()

    def dbg_cap(name, src_ap_fn, prod="V"):
        if dbg and name not in dbg_done:
            dbg_done.add(name)
            ev_ = r.tick(prod)
            r.wait("G", ev_)
            evc = r.g(lambda e: e.dma_start(dbg_d[name], src_ap_fn()), "gdbg")
            for eng in "TVS":
                r.wait(eng, evc)

    last_out_ev = None
    prev_ffn_done_V = None
    prev_ffn_done_S = None
    ev_xm = None
    for b in range(B_PER_CORE):
        r.wait("V", *setup_evs)
        if b == 0:
            r.v(lambda e: e.memset(stats[:], 0.0))
        r.v(lambda e: e.memset(Ssb[:], 0.0))
        ev_sadd = r.tick("V")

        for seg in range(NSEG):
            tok0 = b * L + seg * SEG
            gc0 = seg * NCH

            ev_tr_last = None
            for tt in range(NTT):
                row = tok0 + tt * 128
                r.wait("Q", prev_ffn_done_V)
                evx = r.q(lambda e, row=row: e.dma_start(
                    xt_v(), bass.AP(x_d.tensor, row * D, [[D, 128], [1, D]])), "qx")
                r.wait("S", evx, prev_ffn_done_S, prev_ffn_done_V)
                r.s(lambda e: e.activation(epv(0, 128, 1024), xt_v(), AF.Square,
                                           bias=0.0, scale=1.0 / 32.0))
                ev_sq2 = r.tick("S")
                r.wait("V", evx, prev_ffn_done_V, ev_xm, ev_sq2)
                r.v(lambda e: e.tensor_reduce(stats[:, 2:3], xt_v(), AX.X, OP.add))
                r.v(lambda e: e.tensor_reduce(stats[:, 3:4], epv(0, 128, 1024),
                                              AX.X, OP.add))
                r.v(lambda e: e.tensor_copy(epv(0, 128, 512), xt_v()[:, 0:512]))
                r.v(lambda e: e.tensor_copy(ap(ep, 0, 512, [[0, 128], [1, 512]]),
                                            xt_v()[:, 512:1024]))
                r.v(lambda e: e.tensor_scalar(negm[:], stats[:, 2:3],
                                              -1.0 / 1024.0, None, OP.mult))
                r.v(lambda e: e.tensor_copy(epv(0, 128, 256), xt_v()[:, 0:256]))
                r.v(lambda e: e.tensor_tensor(mv[:, 0:1], negm[:], negm[:], OP.mult))
                r.v(lambda e: e.tensor_copy(ap(ep, 0, 256, [[0, 128], [1, 256]]),
                                            xt_v()[:, 256:512]))
                r.v(lambda e: e.scalar_tensor_tensor(stats[:, 0:1], stats[:, 3:4],
                                                     1.0, mv[:, 0:1],
                                                     OP.mult, OP.subtract))
                ev_var = r.tick("V")
                if dbg and "d_ln0" not in dbg_done and "ln0" not in _DBG_SKIP:
                    r.v(lambda e: e.tensor_copy(ap(ep, 0, 2048, [[0, 128], [1, 4]]),
                                                xt_v()[:, 0:4]))
                    r.v(lambda e: e.tensor_copy(ap(ep, 0, 2052, [[0, 128], [1, 4]]),
                                                stats[:, 0:4]))
                    r.v(lambda e: e.tensor_copy(ap(ep, 0, 2056, [[0, 128], [1, 1]]),
                                                negm[:]))
                    r.v(lambda e: e.tensor_copy(ap(ep, 0, 2057, [[0, 128], [1, 1]]),
                                                mv[:, 0:1]))
                    r.v(lambda e: e.tensor_copy(ap(ep, 0, 2058, [[0, 128], [1, 1]]),
                                                stats[:, 9:10]))
                    r.v(lambda e: e.tensor_copy(ap(ep, 0, 2059, [[0, 128], [1, 1]]),
                                                stats[:, 0:1]))
                    dbg_cap("d_ln0", lambda: ap(ep, 0, 2048, [[0, 128], [1, 12]]))
                r.wait("S", ev_var)
                r.s(lambda e: e.activation(rstd[:], stats[:, 0:1], AF.Sqrt,
                                           bias=EPS, scale=1.0))
                ev_sqr = r.tick("S")
                r.wait("V", ev_sqr)
                r.v(lambda e: e.reciprocal(stats[:, 1:2], rstd[:]))
                r.v(lambda e: e.tensor_copy(stats[:, 9:10], stats[:, 0:1]))
                r.v(lambda e: e.tensor_copy(stats[:, 10:11], stats[:, 0:1]))
                ev_rstd = r.tick("V")
                r.wait("S", ev_rstd)
                r.s(lambda e: e.activation(epv(0, 128, 1024), xt_v(), AF.Identity,
                                           bias=negm[:], scale=1.0))
                ev_xm = r.tick("S")
                r.wait("V", ev_xm, ev_tr_last)
                r.v(lambda e: e.tensor_scalar(xnt[:], epv(0, 128, 1024),
                                              stats[:, 1:2], None, OP.mult))
                ev_xnt = r.tick("V")
                if tt == 0:
                    dbg_cap("d_xn", lambda: xnt[:])
                r.wait("T", ev_xnt, ev_id)
                for k in range(8):
                    r.t(lambda e, k=k: e.matmul(
                        psv(0, 0, 0, [[0, 128], [1, 128]]),
                        xnt[:, k * 128:(k + 1) * 128], ident[:],
                        start=True, stop=True))
                    ev_tr = r.tick("T")
                    r.wait("V", ev_tr)
                    r.v(lambda e, k=k, tt=tt: e.tensor_copy(
                        xn_fm[:, k * 512 + tt * 128: k * 512 + tt * 128 + 128],
                        psv(0, 0, 0, [[0, 128], [1, 128]])))
                    ev_ev = r.tick("V")
                    r.wait("T", ev_ev)
                    ev_tr_last = ev_tr
                r.wait("Q", ev_xm)
            ev_xn_all = r.tick("V")

            r.wait("G", prev_ffn_done_V,
                   ev_xn_all if (b == 0 and seg == 0) else None)
            ev_wv = load_wv()
            r.wait("T", ev_xn_all, ev_wq, ev_wk)
            ev_qk = None
            for which, (wsb, dst0, scale) in enumerate(
                    [(wq_sb, 0, 0.25), (wk_sb, 1024, 1.0)]):
                for half in range(2):
                    for k in range(8):
                        r.t(lambda e, wsb=wsb, half=half, k=k: e.matmul(
                            psv(half, 0, 0, [[0, 128], [1, 512]]),
                            wsb[:, k * 256 + half * 128: k * 256 + half * 128 + 128],
                            xn_fm[:, k * 512: (k + 1) * 512],
                            start=(k == 0), stop=(k == 7)))
                    ev_mm = r.tick("T")
                    r.wait("S", ev_mm)
                    r.s(lambda e, dst0=dst0, half=half, scale=scale: e.activation(
                        qk_fm[:, dst0 + half * 512: dst0 + half * 512 + 512],
                        psv(half, 0, 0, [[0, 128], [1, 512]]),
                        AF.Identity, bias=0.0, scale=scale))
                    ev_qk = r.tick("S")
                    r.wait("T", ev_qk)
            if dbg:
                r.wait("V", ev_qk)
                r.v(lambda e: e.tensor_copy(epv(0, 128, 1024), qk_fm[:, 0:1024]))
                dbg_cap("d_q", lambda: epv(0, 128, 1024))
                r.v(lambda e: e.tensor_copy(epv(2, 128, 1024), qk_fm[:, 1024:2048]))
                ev_dk = r.tick("V")
                dbg_cap("d_k", lambda: epv(2, 128, 1024))

            r.wait("V", prev_ffn_done_V)
            r.v(lambda e: e.memset(bigv(0, V1C + 64, [[0, 64], [1040, 8], [65, 16]]), 1.0))
            ev_ones = r.tick("V")
            r.wait("T", ev_wv, ev_ones, prev_ffn_done_V)
            for c in range(NCH):
                for j in range(2):
                    for k in range(8):
                        r.t(lambda e, c=c, j=j, k=k: e.matmul(
                            psv(2 + j, 0, 0, [[0, 64], [1, 512]]),
                            xn_fm[:, k * 512 + c * 64: k * 512 + c * 64 + 64],
                            ap(pq, 0, 8192 + k * 1024 + j * 512, [[0, 128], [1, 512]]),
                            start=(k == 0), stop=(k == 7)))
                ev_vmm = r.tick("T")
                r.wait("V", ev_vmm)
                for j in range(2):
                    r.v(lambda e, c=c, j=j: e.tensor_copy(
                        bigv(0, V1C + c * 1040 + j * 8 * 65, [[0, 64], [65, 8], [1, 64]]),
                        psv(2 + j, 0, 0, [[0, 64], [1, 512]])))
                ev_v1 = r.tick("V")
                r.wait("T", ev_v1)
            dbg_cap("d_v1", lambda: bigv(0, V1C, [[0, 64], [1, 8 * 1040]]))

            def build_phis(hs, banks, wait_first):
                bb = banks
                evm = {}
                item = 0
                first = True
                for dstbase, src_off, k2 in [(0, 0, 1)]:
                    for h in range(H):
                        half, h8 = divmod(h, 8)
                        for t in range(NTILE):
                            bA1 = bb[2 * (item % 2)]
                            bA2 = bb[2 * (item % 2) + 1]
                            slot = 4 + item % 2
                            rbase = ((h8 * NTILE + t) * 3) * 68
                            if first:
                                r.wait("T", *wait_first)
                                first = False
                            r.wait("T", evm.get(bA1))
                            src = src_off + half * 512 + hs * 256
                            r.t(lambda e, bA1=bA1, rbase=rbase, src=src: e.matmul(
                                psv(bA1, 0, 0, [[0, 68], [1, 256]]),
                                rs_sb[:, rbase: rbase + 68],
                                qk_fm[:, src: src + 256],
                                start=True, stop=True))
                            r.t(lambda e, bA2=bA2, rbase=rbase, k2=k2, src=src: e.matmul(
                                psv(bA2, 0, 0, [[0, 68], [1, 256]]),
                                rs_sb[:, rbase + k2 * 68: rbase + (k2 + 1) * 68],
                                qk_fm[:, src: src + 256],
                                start=True, stop=True))
                            ev_ab = r.tick("T")
                            r.wait("S", ev_ab, evm.get(bA1))
                            r.s(lambda e, bA2=bA2, slot=slot: e.activation(
                                epv(slot, 68, 256), psv(bA2, 0, 0, [[0, 68], [1, 256]]),
                                AF.Identity, bias=0.0, scale=1.0))
                            ev_e2 = r.tick("S")
                            r.wait("V", ev_e2)
                            col = dstbase + (t * H + h) * 256
                            r.v(lambda e, col=col, bA1=bA1, slot=slot: e.tensor_tensor(
                                ap(pq, 0, col, [[0, 68], [1, 256]]),
                                psv(bA1, 0, 0, [[0, 68], [1, 256]]),
                                epv(slot, 68, 256),
                                OP.mult))
                            evm[bA1] = r.tick("V")
                            item += 1
                return r.tick("V")

            def restage(hs, bt1, bt2, wait_first):
                ev_done = None
                first = list(wait_first)
                for which, src0, dst0 in [(0, 0, 0), (1, 1024, 4096)]:
                    for tt2 in range(2):
                        tok = hs * 256 + tt2 * 128
                        r.wait("T", *first)
                        first = []
                        for half in range(2):
                            r.t(lambda e, src0=src0, half=half, tok=tok: e.matmul(
                                psv(bt1, 0, half * 128, [[0, 128], [1, 128]]),
                                qk_fm[:, src0 + half * 512 + tok: src0 + half * 512 + tok + 128],
                                ident[:], start=True, stop=True))
                        ev_t1 = r.tick("T")
                        r.wait("V", ev_t1)
                        r.v(lambda e: e.tensor_copy(
                            tmst[:], psv(bt1, 0, 0, [[0, 128], [1, 256]])))
                        ev_tm = r.tick("V")
                        r.wait("T", ev_tm)
                        for grp in range(4):
                            for hh in range(4):
                                h = grp * 4 + hh
                                r.t(lambda e, h=h, hh=hh: e.matmul(
                                    psv(bt2[hh], 0, 0, [[0, 16], [1, 128]]),
                                    tmst[:, h * 16: h * 16 + 16],
                                    ident[:], start=True, stop=True))
                            ev_t2 = r.tick("T")
                            r.wait("V", ev_t2)
                            bstep = (bt2[1] - bt2[0]) * 512
                            r.v(lambda e, grp=grp, dst0=dst0, tt2=tt2, bstep=bstep: e.tensor_copy(
                                bass.AP(qkh, dst0 + grp * 4 * 256 + tt2 * 128,
                                        [[8192, 16], [256, 4], [1, 128]]),
                                psv(bt2[0], 0, 0, [[0, 16], [bstep, 4], [1, 128]])))
                            ev_done = r.tick("V")
                            r.wait("T", ev_done)
                return ev_done

            ev_rst = restage(0, 0, (4, 5, 6, 7), (ev_qk,))
            dbg_cap("d_qkh", lambda: ap(qkh, 0, 0, [[0, 16], [1, 8192]]))
            ev_phi = build_phis(0, (4, 5, 6, 7), (ev_rs, ev_qk, ev_rst))
            dbg_cap("d_phiqa", lambda: ap(pq, 0, 0, [[0, 68], [1, 16 * 256]]))

            ev_as = {}
            ev_sqw = {}
            ev_sadd_z = ev_sadd
            ev_epi_done = None
            ev_szcast = None
            ev_pkm = {}

            def build_phiktm(cidx, buf, prev_mult_ev):
                first_evac = True
                ev_e = None
                for h in range(H):
                    half, h8 = divmod(h, 8)
                    r.t(lambda e, h8=h8, half=half, cidx=cidx: e.matmul(
                        psv(0, 0, 0, [[0, 64], [1, 136]]),
                        qk_fm[:, 1024 + half * 512 + cidx * 64:
                              1024 + half * 512 + cidx * 64 + 64],
                        rm_sb[:, (h8 * 2) * 136: (h8 * 2) * 136 + 136],
                        start=True, stop=True))
                    r.t(lambda e, h8=h8, half=half, cidx=cidx: e.matmul(
                        psv(3, 0, 180, [[0, 64], [1, 136]]),
                        qk_fm[:, 1024 + half * 512 + cidx * 64:
                              1024 + half * 512 + cidx * 64 + 64],
                        rm_sb[:, (h8 * 2 + 1) * 136: (h8 * 2 + 1) * 136 + 136],
                        start=True, stop=True))
                    ev12 = r.tick("T")
                    r.wait("S", ev12)
                    if first_evac:
                        r.wait("S", prev_mult_ev)
                        first_evac = False
                    r.s(lambda e, h=h: e.activation(
                        ap(pk1, 0, h * 136, [[0, 64], [1, 136]]),
                        psv(0, 0, 0, [[0, 64], [1, 136]]),
                        AF.Identity, bias=0.0, scale=1.0))
                    r.s(lambda e, h=h: e.activation(
                        ap(pk2, 0, h * 136, [[0, 64], [1, 136]]),
                        psv(3, 0, 180, [[0, 64], [1, 136]]),
                        AF.Identity, bias=0.0, scale=1.0))
                    ev_e = r.tick("S")
                    r.wait("T", ev_e)
                r.wait("V", ev_e, prev_mult_ev)
                r.v(lambda e, buf=buf: e.tensor_tensor(
                    phiKtm[buf][:], pk1[:], pk2[:], OP.mult))
                return r.tick("V")

            for c in range(NCH):
                gc = gc0 + c
                if c == 0:
                    r.wait("T", ev_phi, ev_sadd_z, prev_ffn_done_V)
                    ev_pkm[0] = build_phiktm(0, 0, ev_pkm.get(-1))
                if c == 4:
                    ev_rst2 = restage(1, 0, (1, 2, 3, 4),
                                      (ev_epi_done, ev_as[1], ev_sadd_z, ev_pkm[c]))
                    ev_phi = build_phis(1, (0, 1, 2, 3), (ev_rst2,))
                if c == 0 and gc0 == 0:
                    dbg_cap("d_phiktm", lambda: phiKtm[0][:])

                if gc > 0:
                    r.wait("V", ev_sadd_z)
                    r.v(lambda e: e.tensor_copy(SZbf[:], Ssb[:]))
                    ev_szcast = r.tick("V")
                    if gc == 1 and dbg and "d_szc" not in dbg_done and "szc" not in _DBG_SKIP:
                        r.v(lambda e: e.tensor_copy(ap(ep, 0, 0, [[0, 68], [1, 65]]),
                                                    ap(SZbf, 0, 0, [[0, 68], [1, 65]])))
                        r.v(lambda e: e.tensor_copy(ap(ep, 0, 65, [[0, 68], [1, 65]]),
                                                    ap(SZbf, 0, 1040, [[0, 68], [1, 65]])))
                        dbg_cap("d_szc", lambda: ap(ep, 0, 0, [[0, 68], [1, 130]]))

                for wave in range(2):
                    r.wait("T", ev_phi, ev_sqw.get(1 - wave), ev_pkm[c])
                    for hh in range(8):
                        h = wave * 8 + hh
                        r.t(lambda e, h=h, c=c, hh=hh: e.matmul(
                            psv(0, 0, hh * 64, [[0, 64], [1, 64]]),
                            ap(qkh, 0, 4096 + h * 256 + (c % 4) * 64, [[0, 16], [1, 64]]),
                            ap(qkh, 0, h * 256 + (c % 4) * 64, [[0, 16], [1, 64]]),
                            start=True, stop=True))
                    ev_pa = r.tick("T")
                    r.wait("S", ev_pa, ev_as.get(wave))
                    r.s(lambda e, wave=wave: e.activation(
                        epv(wave, 64), psv(0, 0, 0, [[0, 64], [1, 512]]),
                        AF.Square, bias=0.0, scale=1.0))
                    ev_sq = r.tick("S")
                    ev_sqw[wave] = ev_sq
                    r.wait("V", ev_sq, ev_mk)
                    r.v(lambda e, wave=wave: e.tensor_tensor(
                        ap(As_sb, 0, wave * 512, [[0, 64], [1, 512]]),
                        epv(wave, 64),
                        ap(maskrep, 0, wave * 512, [[0, 64], [1, 512]]),
                        OP.mult))
                    ev_as[wave] = r.tick("V")
                ev_as_all = ev_as[1]

                po_banks = [(1, 0, 7), (2, 7, 7), (3, 14, 2)]
                r.wait("T", ev_as_all, ev_szcast, ev_epi_done, ev_sadd_z)
                for bank, h0, nh in po_banks:
                    for hh in range(nh):
                        h = h0 + hh
                        first = True
                        if gc > 0:
                            for t in range(NTILE):
                                r.t(lambda e, bank=bank, hh=hh, h=h, t=t, c=c: e.matmul(
                                    psv(bank, 0, hh * 65, [[0, 64], [1, 65]]),
                                    ap(pq, 0, (t * H + h) * 256 + (c % 4) * 64, [[0, 68], [1, 64]]),
                                    ap(SZbf, 0, (t * H + h) * 65, [[0, 68], [1, 65]]),
                                    start=(t == 0), stop=False))
                            first = False
                        r.t(lambda e, bank=bank, hh=hh, h=h, c=c, first=first: e.matmul(
                            psv(bank, 0, hh * 65, [[0, 64], [1, 65]]),
                            ap(As_sb, 0, h * 64, [[0, 64], [1, 64]]),
                            bigv(0, V1C + c * 1040 + h * 65, [[0, 64], [1, 65]]),
                            start=first, stop=True))
                ev_po = r.tick("T")

                r.wait("V", ev_po)
                if c == 0 and gc0 == 0 and dbg and "d_po" not in dbg_done:
                    r.v(lambda e: e.tensor_copy(epv(2, 64, 512), psv(1, 0, 0, [[0, 64], [1, 512]])))
                    dbg_cap("d_po", lambda: epv(2, 64, 512))
                if c == 0 and gc0 == 0 and dbg and "d_ddr" not in dbg_done:
                    pass
                r.wait("S", ev_po)
                for bank, h0, nh in po_banks:
                    r.s(lambda e, bank=bank, h0=h0, nh=nh: e.activation(
                        ap(dd, 0, h0, [[0, 64], [1, nh]]),
                        psv(bank, 0, 64, [[0, 64], [65, nh]]),
                        AF.Identity, bias=EPS, scale=1.0))
                ev_dd = r.tick("S")
                if gc == gc0 + 1 and gc0 == 0 and dbg and "d_poc1" not in dbg_done and "poc1" not in _DBG_SKIP:
                    r.wait("V", ev_po)
                    r.v(lambda e: e.tensor_copy(epv(2, 64, 130), psv(1, 0, 0, [[0, 64], [1, 130]])))
                    dbg_cap("d_poc1", lambda: epv(2, 64, 130))
                if gc == gc0 + 1 and gc0 == 0 and dbg and "d_ddc1" not in dbg_done and "ddc1" not in _DBG_SKIP:
                    r.wait("V", ev_dd)
                    r.v(lambda e: e.tensor_copy(ap(ep, 0, 1536, [[0, 64], [1, 16]]), dd[:]))
                    dbg_cap("d_ddc1", lambda: ap(ep, 0, 1536, [[0, 64], [1, 16]]))
                r.wait("V", ev_dd)
                r.v(lambda e: e.reciprocal(ddr[:], dd[:]))
                r.v(lambda e: e.tensor_copy(stats[:, 9:10], stats[:, 0:1]))
                r.v(lambda e: e.tensor_copy(stats[:, 10:11], stats[:, 0:1]))
                if c == 0 and gc0 == 0 and dbg and "d_ddr" not in dbg_done:
                    dbg_cap("d_ddr", lambda: ddr[:])
                for bank, h0, nh in po_banks:
                    for hh in range(nh):
                        h = h0 + hh
                        r.v(lambda e, bank=bank, hh=hh, h=h, c=c: e.tensor_scalar(
                            bass.AP(big, OTM + c * 1024 + h * 64, [[33024, 64], [1, 64]]),
                            psv(bank, 0, hh * 65, [[0, 64], [1, 64]]),
                            ddr[:, h:h + 1], None, OP.mult))
                ev_epi_done = r.tick("V")
                if c == 0 and gc0 == 0:
                    dbg_cap("d_dd", lambda: ap(dd, 0, 0, [[0, 64], [1, 16]]))
                    dbg_cap("d_as", lambda: ap(As_sb, 0, 0, [[0, 64], [1, 1024]]))

                SD_BANKS = [(4, 0, 0, 7), (5, 7, 0, 7), (6, 14, 0, 7),
                            (7, 21, 0, 7), (3, 28, 140, 4)]
                r.wait("T", ev_pkm[c], ev_sadd_z)
                for bank, s0, coff, ns in SD_BANKS:
                    if bank == 3:
                        r.wait("T", ev_epi_done)
                    for si in range(ns):
                        s = s0 + si
                        t, h = divmod(s, 16)
                        r.t(lambda e, bank=bank, si=si, coff=coff, t=t, h=h, c=c: e.matmul(
                            psv(bank, 0, coff + si * 65, [[0, 68], [1, 65]]),
                            ap(phiKtm[c % 2], 0, h * 136 + t * 68, [[0, 64], [1, 68]]),
                            bigv(0, V1C + c * 1040 + h * 65, [[0, 64], [1, 65]]),
                            start=True, stop=True))
                ev_sd = r.tick("T")

                r.wait("V", ev_sd)
                for bank, s0, coff, ns in SD_BANKS:
                    r.v(lambda e, bank=bank, s0=s0, coff=coff, ns=ns: e.tensor_tensor(
                        bass.AP(Ssb, s0 * 65, [[2080, 68], [1, ns * 65]]),
                        psv(bank, 0, coff, [[0, 68], [1, ns * 65]]),
                        bass.AP(Ssb, s0 * 65, [[2080, 68], [1, ns * 65]]),
                        OP.add))
                ev_sadd_z = r.tick("V")

                if c + 1 < NCH:
                    r.wait("T", ev_sadd_z, ev_pkm[c])
                    ev_pkm[c + 1] = build_phiktm(c + 1, (c + 1) % 2, ev_pkm[c])
            ev_sadd = ev_sadd_z
            if gc0 == 0:
                dbg_cap("d_otm", lambda: bigv(0, OTM, [[0, 64], [1, 8 * 1024]]))
                dbg_cap("d_sz", lambda: ap(Ssb, 0, 0, [[0, 68], [1, 2080]]))

            r.wait("T", ev_epi_done, ev_sadd, ev_id)
            ev_ofm = None
            for k in range(8):
                for c in range(NCH):
                    r.t(lambda e, k=k, c=c: e.matmul(
                        psv(6, 0, c * 64, [[0, 128], [1, 64]]),
                        bass.AP(big, OTM + c * 1024 + k * 128, [[33024, 64], [1, 128]]),
                        ident[0:64, 0:64], start=True, stop=True))
                ev_tr = r.tick("T")
                r.wait("V", ev_tr)
                r.v(lambda e, k=k: e.tensor_copy(
                    o_fm_v(k * 512, 512),
                    psv(6, 0, 0, [[0, 128], [1, 512]])))
                ev_ofm = r.tick("V")
                r.wait("T", ev_ofm)

            ev_w1 = {0: load_w1(0, 0)}
            ev_w1_mm_done = {}
            ev_z0_by_bank = {}
            ev_hb_prev = None
            ev_y2_prev = None
            for fft in range(32):
                bank = fft % 2
                if fft + 1 < 32:
                    if fft >= 1:
                        r.wait("G", ev_w1_mm_done[fft - 1])
                    ev_w1[fft + 1] = load_w1((fft + 1) % 2, fft + 1)
                r.wait("T", ev_w1[fft], ev_z0_by_bank.get(bank), ev_b1s)
                for k in range(8):
                    r.t(lambda e, fft=fft, k=k, bank=bank: e.matmul(
                        psv(bank, 0, 0, [[0, 128], [1, 512]]),
                        w1_sb[fft % 2][:, k * 128: k * 128 + 128],
                        xn_fm[:, k * 512: (k + 1) * 512],
                        start=(k == 0), stop=(k == 7)))
                ev_h = r.tick("T")
                ev_w1_mm_done[fft] = ev_h
                r.wait("S", ev_h, ev_hb_prev, ev_y2_prev)
                r.s(lambda e, bank=bank, fft=fft: e.activation(
                    epv(0), psv(bank, 0, 0, [[0, 128], [1, 512]]),
                    AF.Identity, bias=b1s[:, fft:fft + 1], scale=256.0))
                ev_z0 = r.tick("S")
                ev_z0_by_bank[bank] = ev_z0
                r.wait("V", ev_z0, ev_y2_prev)
                r.v(lambda e: e.tensor_scalar(epv(1), epv(0), MAGIC,
                                              MAGIC - 1024.0, OP.add, OP.max))
                r.v(lambda e: e.tensor_scalar(epv(2), epv(1), MAGIC + 1024.0,
                                              -(MAGIC - 1024.0), OP.min, OP.add))
                ev_vp = r.tick("V")
                r.wait("S", ev_vp)
                r.s(lambda e: e.activation(epv(3), epv(2), AF.Identity,
                                           bias=0.0, scale=1.0 / 64.0))
                r.s(lambda e: e.activation(epi[:], epv(2), AF.Identity,
                                           bias=0.0, scale=1.0 / 64.0))
                ev_i0 = r.tick("S")
                r.wait("V", ev_i0)
                r.v(lambda e: e.tensor_tensor(epv(4), epi[:], epv(3), OP.is_gt))
                r.v(lambda e: e.tensor_tensor(epv(2), epi[:], epv(4), OP.subtract))
                r.v(lambda e: e.tensor_tensor(epv(3), epv(3), epv(2), OP.subtract))
                ev_it = r.tick("V")
                r.wait("S", ev_it)
                r.s(lambda e: e.activation(epv(4), epv(2), AF.Gelu,
                                           bias=-4.0, scale=0.25))
                r.s(lambda e: e.activation(epv(5), epv(2), AF.Gelu,
                                           bias=-3.75, scale=0.25))
                ev_g = r.tick("S")
                r.wait("V", ev_g)
                r.v(lambda e: e.tensor_tensor(epv(5), epv(5), epv(4), OP.subtract))
                r.v(lambda e: e.tensor_tensor(epv(5), epv(3), epv(5), OP.mult))
                r.v(lambda e: e.tensor_tensor(epv(4), epv(5), epv(4), OP.add))
                r.v(lambda e: e.tensor_scalar(epi[:], epv(1), MAGIC + 1024.0,
                                              None, OP.is_gt))
                r.v(lambda e: e.tensor_scalar(epv(2), epv(1), -MAGIC,
                                              1.0 / 256.0, OP.add, OP.mult))
                r.v(lambda e: e.select(epv(3), epi[:], epv(2), epv(4)))
                r.v(lambda e: e.tensor_scalar(epv(5), epv(0), -1024.5,
                                              None, OP.is_ge))
                r.v(lambda e: e.tensor_tensor(epv(4), epv(3), epv(5), OP.mult))
                ev_y2 = r.tick("V")
                ev_y2_prev = ev_y2
                r.wait("S", ev_y2)
                r.s(lambda e: e.activation(epv(5), epv(4), AF.Identity,
                                           bias=MAGIC + 1024.0, scale=256.0))
                r.s(lambda e, fft=fft: e.activation(
                    bass.AP(big, H_COLS + fft * 512, [[33024, 128], [1, 512]]),
                    epv(5), AF.Identity, bias=-49156.0, scale=1.0 / 256.0))
                ev_hb_prev = r.tick("S")
                if fft == 0 and gc0 == 0 and b == 0:
                    dbg_cap("d_h", lambda: epv(4))
            ev_h_all = ev_hb_prev

            ev_w2 = {(0, 0): load_w2(0, 0, 0)}
            r.wait("G", ev_sadd)
            ev_wo_dh = {0: load_wo(0)}
            ev_out = None
            ev_z4_prev = None
            prev_store_any = None
            ev_mm2_read = {}
            ev_mm2_sl_prev = None
            for dh in range(2):
                for sl in range(16):
                    nxt = (dh, sl + 1) if sl + 1 < 16 else ((1, 0) if dh == 0 else None)
                    if nxt is not None:
                        if (dh, sl) != (0, 0):
                            r.wait("G", ev_mm2_sl_prev)
                        ev_w2[nxt] = load_w2((sl + 1 + dh * 16) % 2, nxt[0], nxt[1])
                    r.wait("T", ev_w2[(dh, sl)],
                           ev_hb_prev if (dh == 0 and sl == 0) else None,
                           ev_mm2_read.get(dh - 1))
                    for ffcl in range(2):
                        ffc = sl * 2 + ffcl
                        for tt in range(NTT):
                            r.t(lambda e, dh=dh, sl=sl, ffcl=ffcl, ffc=ffc, tt=tt: e.matmul(
                                psv(2 + tt, 0, 0, [[0, 128], [1, 512]]),
                                bass.AP(big, H_COLS + ffc * 512 + tt * 128,
                                        [[33024, 128], [1, 128]]),
                                w2_sb[(sl + dh * 16) % 2][:, ffcl * 512: ffcl * 512 + 512],
                                start=(sl == 0 and ffcl == 0), stop=(sl == 15 and ffcl == 1)))
                    ev_mm2_sl_prev = r.tick("T")
                ev_mm2 = ev_mm2_sl_prev
                for tt in range(NTT):
                    row = tok0 + tt * 128
                    r.wait("Q", ev_y2_prev, ev_out)
                    evxr = r.q(lambda e, row=row: e.dma_start(
                        xr_v(), bass.AP(x_d.tensor, row * D, [[D, 128], [1, D]])), "qx")
                    r.wait("V", ev_mm2, ev_b2t, prev_store_any, ev_z4_prev)
                    r.v(lambda e, tt=tt, dh=dh: e.scalar_tensor_tensor(
                        osb[0][:], psv(2 + tt, 0, 0, [[0, 128], [1, 512]]),
                        256.0, b2t[:, dh * 512: dh * 512 + 512], OP.mult, OP.add))
                    ev_z3 = r.tick("V")
                    r.wait("S", ev_z3, ev_out)
                    r.s(lambda e: e.activation(u4_v(), osb[0][:], AF.Identity,
                                               bias=MAGIC, scale=1.0))
                    ev_z4_prev = r.tick("S")
                    r.s(lambda e: e.activation(q3_v(), u4_v(), AF.Identity,
                                               bias=-49152.0, scale=1.0 / 256.0))
                    ev_q3 = r.tick("S")
                    r.wait("T", ev_z3, ev_ofm, ev_wo_dh[dh])
                    for k in range(8):
                        r.t(lambda e, k=k, tt=tt: e.matmul(
                            psv(2 + tt, 0, 0, [[0, 128], [1, 512]]),
                            o_fm_v(k * 512 + tt * 128, 128),
                            ap(pq, 0, 8192 + k * 512, [[0, 128], [1, 512]]),
                            start=(k == 0), stop=(k == 7)))
                    ev_wom = r.tick("T")
                    if tt == NTT - 1 and dh == 0:
                        r.wait("G", ev_wom)
                        ev_wo_dh[1] = load_wo(1)
                    r.wait("V", ev_wom, evxr, ev_q3)
                    r.v(lambda e, dh=dh, tt=tt: e.tensor_tensor(
                        osb[1][:], xr_h(dh),
                        psv(2 + tt, 0, 0, [[0, 128], [1, 512]]), OP.add))
                    r.v(lambda e: e.tensor_tensor(osb[1][:], osb[1][:], q3_v(), OP.add))
                    ev_out = r.tick("V")
                    r.wait("Q", ev_out)
                    ev_st = r.q(lambda e, row=row, dh=dh: e.dma_start(
                        bass.AP(out_d.tensor, row * D + dh * 512, [[D, 128], [1, 512]]),
                        osb[1][:]), "qout")
                    prev_store_any = ev_st
                    last_out_ev = ev_st
                ev_mm2_read[dh] = ev_out
            prev_ffn_done_V = ev_out
            prev_ffn_done_S = ev_q3

    r.final_waits("Q")
    sems = {}
    for e in list("TVS") + sorted(r.streams):
        sems[e] = es.enter_context(nc.semaphore(f"sem{e}"))
    with es:
        r.replay(nc, sems)
    return nc


def kernel(L0_attn_hidden_states, L0_attn_Wq, L0_attn_Wk, L0_attn_Wv, L0_attn_Wo,
           L0_ffn_w1, L0_ffn_b1, L0_ffn_w2, L0_ffn_b2):
    x = np.ascontiguousarray(np.asarray(L0_attn_hidden_states, np.float32))
    consts = host_consts()
    shared = dict(
        wq=np.ascontiguousarray(L0_attn_Wq, np.float32),
        wk=np.ascontiguousarray(L0_attn_Wk, np.float32),
        wv=np.ascontiguousarray(L0_attn_Wv, np.float32),
        wo=np.ascontiguousarray(L0_attn_Wo, np.float32),
        w1=np.ascontiguousarray(L0_ffn_w1, np.float32),
        b1=np.ascontiguousarray(L0_ffn_b1, np.float32),
        w2=np.ascontiguousarray(L0_ffn_w2, np.float32),
        b2=np.ascontiguousarray(L0_ffn_b2, np.float32),
        **consts)
    in_maps = []
    for c in range(NCORES):
        m = dict(shared)
        m["x"] = np.ascontiguousarray(
            x[c * B_PER_CORE:(c + 1) * B_PER_CORE].reshape(B_PER_CORE * L, D))
        in_maps.append(m)
    nc = build_nc(dbg=_DEBUG)
    res = run_bass_kernel_spmd(nc, in_maps, core_ids=list(range(NCORES)))
    out = np.concatenate([r["out"].reshape(B_PER_CORE, L, D) for r in res.results],
                         axis=0)
    if _DEBUG:
        kernel._dbg = list(res.results)
    return out


# version 07
# speedup vs baseline: 1.0092x; 1.0092x over previous
import contextlib
import numpy as np
import ml_dtypes
import concourse.bass as bass
import concourse.mybir as mybir
from concourse.bass_utils import run_bass_kernel_spmd

F32 = mybir.dt.float32
BF16 = mybir.dt.bfloat16
I32 = mybir.dt.int32
AF = mybir.ActivationFunctionType
OP = mybir.AluOpType
AX = mybir.AxisListType

MAGIC = 12582912.0
EPS = 1e-5
NCORES = 8
B_PER_CORE = 2
L = 4096
D = 1024
FF = 4096
H = 16
CH = 64
SEG = 512
NSEG = L // SEG
NCH = SEG // CH
NTT = SEG // 128

PAIRS = [(i, j) for i in range(16) for j in range(i, 16)]
TA, TB = PAIRS[:68], PAIRS[68:]
NTILE = 2

_DEBUG = False
_DBG_SKIP = frozenset()

CONSTS = [MAGIC, MAGIC + 1024.0, MAGIC - 1024.0, -4.0, -3.75, 256.0,
          1.0 / 64.0, 1.0 / 256.0, 0.25, -49152.0, -49156.0, EPS,
          -1024.5, -(MAGIC - 1024.0)]


def _bf(x):
    return np.asarray(x, dtype=np.float32).astype(ml_dtypes.bfloat16)


def host_consts():
    d = {}
    d["cpool"] = np.tile(np.array(CONSTS, np.float32), (128, 1))
    rs = np.zeros((128, 8 * NTILE * 3 * 68), np.float32)
    for h8 in range(8):
        for t in range(NTILE):
            pairs = (TA, TB)[t]
            for kind in range(3):
                base = ((h8 * NTILE + t) * 3 + kind) * 68
                for pp, (i, j) in enumerate(pairs):
                    if kind == 0:
                        rs[h8 * 16 + i, base + pp] = 1.0
                    elif kind == 1:
                        rs[h8 * 16 + j, base + pp] = 1.0
                    else:
                        rs[h8 * 16 + j, base + pp] = 2.0 if i != j else 1.0
    d["rs"] = _bf(rs)
    rm = np.zeros((128, 8 * 2 * 136), np.float32)
    for h8 in range(8):
        for kind in range(2):
            base = (h8 * 2 + kind) * 136
            for p, (i, j) in enumerate(PAIRS):
                if kind == 0:
                    rm[h8 * 16 + i, base + p] = 1.0
                else:
                    rm[h8 * 16 + j, base + p] = 2.0 if i != j else 1.0
    d["rm"] = _bf(rm)
    m = np.zeros((64, H * 64), np.float32)
    tril = (np.arange(64)[:, None] <= np.arange(64)[None, :]).astype(np.float32)
    for h in range(H):
        m[:, h * 64:(h + 1) * 64] = tril
    d["maskrep"] = _bf(m)
    d["ident"] = _bf(np.eye(128, dtype=np.float32))
    return d


class Rec:
    ENG = "TVSQG"

    def __init__(self):
        self.ops = {e: [] for e in self.ENG}
        self.cnt = {}
        self.waited = {}
        self.streams = set()

    def emit(self, e, fn):
        self.ops[e].append(("i", [fn, 0, e]))
        return self

    def t(self, fn):
        return self.emit("T", fn)

    def v(self, fn):
        return self.emit("V", fn)

    def s(self, fn):
        return self.emit("S", fn)

    def q(self, fn, stream):
        self.streams.add(stream)
        self.ops["Q"].append(("i", [fn, 16, stream]))
        self.cnt[stream] = self.cnt.get(stream, 0) + 16
        return (stream, self.cnt[stream])

    def g(self, fn, stream):
        self.streams.add(stream)
        self.ops["G"].append(("i", [fn, 16, stream]))
        self.cnt[stream] = self.cnt.get(stream, 0) + 16
        return (stream, self.cnt[stream])

    def tick(self, e):
        for kind, ent in reversed(self.ops[e]):
            if kind == "i":
                if ent[1] == 0:
                    ent[1] = 1
                    self.cnt[e] = self.cnt.get(e, 0) + 1
                return (e, self.cnt[e])
        raise RuntimeError(f"no instruction on {e}")

    def final_waits(self, e):
        for st in sorted(self.streams):
            self.ops[e].append(("w", (st, self.cnt.get(st, 0))))

    def wait(self, e, *evs):
        for ev in evs:
            if ev is None:
                continue
            pe, v = ev
            if pe == e:
                continue
            k = (e, pe)
            if self.waited.get(k, 0) >= v:
                continue
            self.waited[k] = v
            self.ops[e].append(("w", (pe, v)))

    def replay(self, nc, sems):
        def run_engine(e, eng):
            for kind, payload in self.ops[e]:
                if kind == "w":
                    pe, v = payload
                    eng.wait_ge(sems[pe], v)
                else:
                    fn, inc, sem = payload
                    ins = fn(eng)
                    if inc:
                        ins.then_inc(sems[sem], inc)

        with nc.Block() as block:
            @block.tensor
            def _(eng):
                run_engine("T", eng)

            @block.vector
            def _(eng):
                run_engine("V", eng)

            @block.scalar
            def _(eng):
                run_engine("S", eng)

            @block.sync
            def _(eng):
                run_engine("Q", eng)

            @block.gpsimd
            def _(eng):
                run_engine("G", eng)


def ap(t, poff, coff, dims):
    rw = t.shape[1]
    return bass.AP(t, poff * rw + coff, [[rw, dims[0][1]]] + list(dims[1:]))


def build_nc(dbg=False):
    nc = bass.Bass("TRN2", target_bir_lowering=False,
                   detect_race_conditions=False)
    x_d = nc.dram_tensor("x", [B_PER_CORE * L, D], F32, kind="ExternalInput").ap()
    wq_d = nc.dram_tensor("wq", [D, 256], F32, kind="ExternalInput").ap()
    wk_d = nc.dram_tensor("wk", [D, 256], F32, kind="ExternalInput").ap()
    wv_d = nc.dram_tensor("wv", [D, D], F32, kind="ExternalInput").ap()
    wo_d = nc.dram_tensor("wo", [D, D], F32, kind="ExternalInput").ap()
    w1_d = nc.dram_tensor("w1", [D, FF], F32, kind="ExternalInput").ap()
    b1_d = nc.dram_tensor("b1", [FF], F32, kind="ExternalInput").ap()
    w2_d = nc.dram_tensor("w2", [FF, D], F32, kind="ExternalInput").ap()
    b2_d = nc.dram_tensor("b2", [D], F32, kind="ExternalInput").ap()
    cp_d = nc.dram_tensor("cpool", [128, len(CONSTS)], F32, kind="ExternalInput").ap()
    rs_d = nc.dram_tensor("rs", [128, 8 * NTILE * 3 * 68], BF16, kind="ExternalInput").ap()
    rm_d = nc.dram_tensor("rm", [128, 8 * 2 * 136], BF16, kind="ExternalInput").ap()
    mk_d = nc.dram_tensor("maskrep", [64, H * 64], BF16, kind="ExternalInput").ap()
    id_d = nc.dram_tensor("ident", [128, 128], BF16, kind="ExternalInput").ap()
    out_d = nc.dram_tensor("out", [B_PER_CORE * L, D], F32, kind="ExternalOutput").ap()
    dbg_d = {}
    if dbg:
        for name, shape in [("d_xn", [128, D]), ("d_q", [128, 1024]),
                            ("d_k", [128, 1024]), ("d_v1", [64, 8 * 1040]),
                            ("d_phiqa", [68, 16 * 256]), ("d_phikfa", [68, 16 * 256]),
                            ("d_phiktm", [64, 16 * 136]), ("d_otm", [64, 8 * 1024]),
                            ("d_h", [128, 512]), ("d_sz", [68, 2080]),
                            ("d_dd", [64, 16]), ("d_as", [64, 1024]),
                            ("d_qkh", [16, 8192]), ("d_po", [64, 512]), ("d_ddr", [64, 16]),
                            ("d_szc", [68, 130]), ("d_poc1", [64, 130]), ("d_ddc1", [64, 16]),
                            ("d_ln0", [128, 12])]:
            dbg_d[name] = nc.dram_tensor(name, shape, F32, kind="ExternalOutput").ap()

    es = contextlib.ExitStack()
    sb = lambda name, shape, dt=F32: es.enter_context(nc.sbuf_tensor(name, shape, dt))

    cpool = sb("cpool_sb", [128, len(CONSTS)])
    wq_sb = sb("wq_sb", [128, 2048], BF16)
    wk_sb = sb("wk_sb", [128, 2048], BF16)
    w1_sb = [sb(f"w1_sb{i}", [128, 1024], BF16) for i in range(2)]
    w2_sb = [sb(f"w2_sb{i}", [128, 1024], BF16) for i in range(2)]
    b1s = sb("b1s", [128, 32])
    b2t = sb("b2t", [128, D], BF16)
    rs_sb = sb("rs_sb", [128, 8 * NTILE * 3 * 68], BF16)
    rm_sb = sb("rm_sb", [128, 8 * 2 * 136], BF16)
    maskrep = sb("maskrep_sb", [64, H * 64], BF16)
    ident = sb("ident_sb", [128, 128], BF16)
    xnt = sb("xnt", [128, D], BF16)
    stats = sb("stats", [128, 12])
    mv = sb("mv", [128, 2])
    negm = sb("negm", [128, 1])
    rstd = sb("rstd", [128, 1])
    osb = [sb(f"osb{i}", [128, 512]) for i in range(2)]
    xn_fm = sb("xn_fm", [128, 8 * SEG], BF16)
    qk_fm = sb("qk_fm", [128, 2 * 1024], BF16)
    pq = sb("pq", [128, 16384], BF16)
    qkh = sb("qkh", [16, 8192], BF16)
    tmst = sb("tmst", [128, 256], BF16)
    phiKtm_t = sb("phiKtm", [64, H * 136], BF16)
    phiKtm = [phiKtm_t, phiKtm_t]
    pk1 = sb("pk1", [64, H * 136], BF16)
    pk2 = sb("pk2", [64, H * 136], BF16)
    As_sb = sb("As_sb", [64, H * 64], BF16)
    Ssb = sb("Ssb", [68, NTILE * H * 65])
    SZbf = sb("SZbf", [68, NTILE * H * 65], BF16)
    big = sb("big", [128, 33024], BF16)
    dd = sb("dd", [64, 16])
    ddr = sb("ddr", [64, 16])
    ep = sb("ep", [128, 6 * 512])
    epi = sb("epi", [128, 512], I32)
    u4 = None
    q3v = None

    PS = es.enter_context(nc.psum_tensor("PS", [128, 4096], F32))

    H_COLS = 0
    OTM = 16384
    V1C = 24576

    def bigv(poff, coff, dims):
        return ap(big, poff, coff, dims)

    def o_fm_v(c0, w):
        return ap(big, 0, V1C + c0, [[0, 128], [1, w]])

    def psv(bank, poff, coff, dims):
        return ap(PS, poff, bank * 512 + coff, dims)

    def epv(i, rows=128, w=512):
        return ap(ep, 0, i * 512, [[0, rows], [1, w]])

    xr_v = lambda: ap(ep, 0, 1024, [[0, 128], [1, 1024]])
    xt_v = xr_v
    xr_h = lambda dh: ap(ep, 0, 1024 + dh * 512, [[0, 128], [1, 512]])
    u4_v = lambda: epv(0)
    q3_v = lambda: epv(1)

    for i, val in enumerate(CONSTS):
        nc.const_aps.register(F32, val, cpool[:, i:i + 1])

    r = Rec()

    r.q(lambda e: e.dma_start(cpool[:], cp_d), "qset")
    r.q(lambda e: e.dma_start(rs_sb[:], rs_d), "qset")
    r.q(lambda e: e.dma_start(rm_sb[:], rm_d), "qset")
    r.q(lambda e: e.dma_start(maskrep[:], mk_d), "qset")
    r.q(lambda e: e.dma_start(ident[:], id_d), "qset")
    r.q(lambda e: e.dma_start(ap(ep, 0, 0, [[0, 128], [1, 32]]),
                              bass.AP(b1_d.tensor, 0, [[1, 128], [128, 32]]),
                              allow_slow_non_contiguous=True), "qset")
    ev_setup_q = r.q(lambda e: e.dma_start(ap(ep, 0, 2048, [[0, 1], [1, D]]),
                                           bass.AP(b2_d.tensor, 0, [[D, 1], [1, D]])), "qset")
    r.wait("V", ev_setup_q)
    r.v(lambda e: e.tensor_scalar(b1s[:], ap(ep, 0, 0, [[0, 128], [1, 32]]),
                                  256.0, None, OP.mult))
    ev_b1s = r.tick("V")
    r.v(lambda e: e.tensor_scalar(ap(big, 0, 0, [[0, 1], [1, D]]),
                                  ap(ep, 0, 2048, [[0, 1], [1, D]]),
                                  256.0, None, OP.mult))
    r.v(lambda e: e.memset(ap(big, 0, 1024, [[0, 1], [1, 128]]), 1.0))
    ev_b2c = r.tick("V")
    r.wait("T", ev_b2c)
    for jj in range(2):
        r.t(lambda e, jj=jj: e.matmul(
            psv(jj, 0, 0, [[0, 128], [1, 512]]),
            ap(big, 0, 1024, [[0, 1], [1, 128]]),
            ap(big, 0, jj * 512, [[0, 1], [1, 512]]),
            start=True, stop=True))
    ev_b2m = r.tick("T")
    r.wait("V", ev_b2m)
    for jj in range(2):
        r.v(lambda e, jj=jj: e.tensor_copy(b2t[:, jj * 512:(jj + 1) * 512],
                                           psv(jj, 0, 0, [[0, 128], [1, 512]])))
    ev_b2t = r.tick("V")
    ev_wq = r.g(lambda e: e.dma_start(
        wq_sb[:], bass.AP(wq_d.tensor, 0, [[256, 128], [128 * 256, 8], [1, 256]])), "gwq")
    ev_wk = r.g(lambda e: e.dma_start(
        wk_sb[:], bass.AP(wk_d.tensor, 0, [[256, 128], [128 * 256, 8], [1, 256]])), "gwk")
    ev_id = ev_mk = ev_rs = ev_setup_q
    setup_evs = [ev_setup_q, ev_b1s, ev_b2t, ev_wq, ev_wk]

    def load_w1(buf, fft):
        return r.g(lambda e: e.dma_start(
            w1_sb[buf][:], bass.AP(w1_d.tensor, fft * 128,
                                   [[FF, 128], [128 * FF, 8], [1, 128]])), f"gw1{buf}")

    def load_w2(buf, dh, sl):
        return r.g(lambda e: e.dma_start(
            w2_sb[buf][:], bass.AP(w2_d.tensor, (sl * 256) * D + dh * 512,
                                   [[D, 128], [128 * D, 2], [1, 512]])), f"gw2{buf}")

    def load_wo(dh):
        return r.g(lambda e: e.dma_start(
            ap(pq, 0, 8192, [[0, 128], [512, 8], [1, 512]]),
            bass.AP(wo_d.tensor, dh * 512, [[D, 128], [128 * D, 8], [1, 512]])), "gwo")

    def load_wv():
        return r.g(lambda e: e.dma_start(
            ap(pq, 0, 8192, [[0, 128], [1024, 8], [1, 1024]]),
            bass.AP(wv_d.tensor, 0, [[D, 128], [128 * D, 8], [1, D]])), "gwv")

    dbg_done = set()

    def dbg_cap(name, src_ap_fn, prod="V"):
        if dbg and name not in dbg_done:
            dbg_done.add(name)
            ev_ = r.tick(prod)
            r.wait("G", ev_)
            evc = r.g(lambda e: e.dma_start(dbg_d[name], src_ap_fn()), "gdbg")
            for eng in "TVS":
                r.wait(eng, evc)

    last_out_ev = None
    prev_ffn_done_V = None
    prev_ffn_done_S = None
    ev_xm = None
    for b in range(B_PER_CORE):
        r.wait("V", *setup_evs)
        if b == 0:
            r.v(lambda e: e.memset(stats[:], 0.0))
        r.v(lambda e: e.memset(Ssb[:], 0.0))
        ev_sadd = r.tick("V")

        for seg in range(NSEG):
            tok0 = b * L + seg * SEG
            gc0 = seg * NCH

            ev_tr_last = None
            for tt in range(NTT):
                row = tok0 + tt * 128
                r.wait("Q", prev_ffn_done_V)
                evx = r.q(lambda e, row=row: e.dma_start(
                    xt_v(), bass.AP(x_d.tensor, row * D, [[D, 128], [1, D]])), "qx")
                r.wait("S", evx, prev_ffn_done_S, prev_ffn_done_V)
                r.s(lambda e: e.activation(epv(0, 128, 1024), xt_v(), AF.Square,
                                           bias=0.0, scale=1.0 / 32.0))
                ev_sq2 = r.tick("S")
                r.wait("V", evx, prev_ffn_done_V, ev_xm, ev_sq2)
                r.v(lambda e: e.tensor_reduce(stats[:, 2:3], xt_v(), AX.X, OP.add))
                r.v(lambda e: e.tensor_reduce(stats[:, 3:4], epv(0, 128, 1024),
                                              AX.X, OP.add))
                r.v(lambda e: e.tensor_copy(epv(0, 128, 512), xt_v()[:, 0:512]))
                r.v(lambda e: e.tensor_copy(ap(ep, 0, 512, [[0, 128], [1, 512]]),
                                            xt_v()[:, 512:1024]))
                r.v(lambda e: e.tensor_scalar(negm[:], stats[:, 2:3],
                                              -1.0 / 1024.0, None, OP.mult))
                r.v(lambda e: e.tensor_copy(epv(0, 128, 256), xt_v()[:, 0:256]))
                r.v(lambda e: e.tensor_tensor(mv[:, 0:1], negm[:], negm[:], OP.mult))
                r.v(lambda e: e.tensor_copy(ap(ep, 0, 256, [[0, 128], [1, 256]]),
                                            xt_v()[:, 256:512]))
                r.v(lambda e: e.scalar_tensor_tensor(stats[:, 0:1], stats[:, 3:4],
                                                     1.0, mv[:, 0:1],
                                                     OP.mult, OP.subtract))
                ev_var = r.tick("V")
                if dbg and "d_ln0" not in dbg_done and "ln0" not in _DBG_SKIP:
                    r.v(lambda e: e.tensor_copy(ap(ep, 0, 2048, [[0, 128], [1, 4]]),
                                                xt_v()[:, 0:4]))
                    r.v(lambda e: e.tensor_copy(ap(ep, 0, 2052, [[0, 128], [1, 4]]),
                                                stats[:, 0:4]))
                    r.v(lambda e: e.tensor_copy(ap(ep, 0, 2056, [[0, 128], [1, 1]]),
                                                negm[:]))
                    r.v(lambda e: e.tensor_copy(ap(ep, 0, 2057, [[0, 128], [1, 1]]),
                                                mv[:, 0:1]))
                    r.v(lambda e: e.tensor_copy(ap(ep, 0, 2058, [[0, 128], [1, 1]]),
                                                stats[:, 9:10]))
                    r.v(lambda e: e.tensor_copy(ap(ep, 0, 2059, [[0, 128], [1, 1]]),
                                                stats[:, 0:1]))
                    dbg_cap("d_ln0", lambda: ap(ep, 0, 2048, [[0, 128], [1, 12]]))
                r.wait("S", ev_var)
                r.s(lambda e: e.activation(rstd[:], stats[:, 0:1], AF.Sqrt,
                                           bias=EPS, scale=1.0))
                ev_sqr = r.tick("S")
                r.wait("V", ev_sqr)
                r.v(lambda e: e.reciprocal(stats[:, 1:2], rstd[:]))
                r.v(lambda e: e.tensor_copy(stats[:, 9:10], stats[:, 0:1]))
                r.v(lambda e: e.tensor_copy(stats[:, 10:11], stats[:, 0:1]))
                ev_rstd = r.tick("V")
                r.wait("S", ev_rstd)
                r.s(lambda e: e.activation(epv(0, 128, 1024), xt_v(), AF.Identity,
                                           bias=negm[:], scale=1.0))
                ev_xm = r.tick("S")
                r.wait("V", ev_xm, ev_tr_last)
                r.v(lambda e: e.tensor_scalar(xnt[:], epv(0, 128, 1024),
                                              stats[:, 1:2], None, OP.mult))
                ev_xnt = r.tick("V")
                if tt == 0:
                    dbg_cap("d_xn", lambda: xnt[:])
                r.wait("T", ev_xnt, ev_id)
                for k in range(8):
                    r.t(lambda e, k=k: e.matmul(
                        psv(0, 0, 0, [[0, 128], [1, 128]]),
                        xnt[:, k * 128:(k + 1) * 128], ident[:],
                        start=True, stop=True))
                    ev_tr = r.tick("T")
                    r.wait("V", ev_tr)
                    r.v(lambda e, k=k, tt=tt: e.tensor_copy(
                        xn_fm[:, k * 512 + tt * 128: k * 512 + tt * 128 + 128],
                        psv(0, 0, 0, [[0, 128], [1, 128]])))
                    ev_ev = r.tick("V")
                    r.wait("T", ev_ev)
                    ev_tr_last = ev_tr
                r.wait("Q", ev_xm)
            ev_xn_all = r.tick("V")

            r.wait("G", prev_ffn_done_V,
                   ev_xn_all if (b == 0 and seg == 0) else None)
            ev_wv = load_wv()
            r.wait("T", ev_xn_all, ev_wq, ev_wk)
            ev_qk = None
            for which, (wsb, dst0, scale) in enumerate(
                    [(wq_sb, 0, 0.25), (wk_sb, 1024, 1.0)]):
                for half in range(2):
                    for k in range(8):
                        r.t(lambda e, wsb=wsb, half=half, k=k: e.matmul(
                            psv(half, 0, 0, [[0, 128], [1, 512]]),
                            wsb[:, k * 256 + half * 128: k * 256 + half * 128 + 128],
                            xn_fm[:, k * 512: (k + 1) * 512],
                            start=(k == 0), stop=(k == 7)))
                    ev_mm = r.tick("T")
                    r.wait("S", ev_mm)
                    r.s(lambda e, dst0=dst0, half=half, scale=scale: e.activation(
                        qk_fm[:, dst0 + half * 512: dst0 + half * 512 + 512],
                        psv(half, 0, 0, [[0, 128], [1, 512]]),
                        AF.Identity, bias=0.0, scale=scale))
                    ev_qk = r.tick("S")
                    r.wait("T", ev_qk)
            if dbg:
                r.wait("V", ev_qk)
                r.v(lambda e: e.tensor_copy(epv(0, 128, 1024), qk_fm[:, 0:1024]))
                dbg_cap("d_q", lambda: epv(0, 128, 1024))
                r.v(lambda e: e.tensor_copy(epv(2, 128, 1024), qk_fm[:, 1024:2048]))
                ev_dk = r.tick("V")
                dbg_cap("d_k", lambda: epv(2, 128, 1024))

            r.wait("V", prev_ffn_done_V)
            r.v(lambda e: e.memset(bigv(0, V1C + 64, [[0, 64], [1040, 8], [65, 16]]), 1.0))
            ev_ones = r.tick("V")
            r.wait("T", ev_wv, ev_ones, prev_ffn_done_V)
            for c in range(NCH):
                for j in range(2):
                    for k in range(8):
                        r.t(lambda e, c=c, j=j, k=k: e.matmul(
                            psv(2 + j, 0, 0, [[0, 64], [1, 512]]),
                            xn_fm[:, k * 512 + c * 64: k * 512 + c * 64 + 64],
                            ap(pq, 0, 8192 + k * 1024 + j * 512, [[0, 128], [1, 512]]),
                            start=(k == 0), stop=(k == 7)))
                ev_vmm = r.tick("T")
                r.wait("V", ev_vmm)
                for j in range(2):
                    r.v(lambda e, c=c, j=j: e.tensor_copy(
                        bigv(0, V1C + c * 1040 + j * 8 * 65, [[0, 64], [65, 8], [1, 64]]),
                        psv(2 + j, 0, 0, [[0, 64], [1, 512]])))
                ev_v1 = r.tick("V")
                r.wait("T", ev_v1)
            dbg_cap("d_v1", lambda: bigv(0, V1C, [[0, 64], [1, 8 * 1040]]))

            def build_phis(hs, banks, wait_first):
                bb = banks
                evm = {}
                item = 0
                first = True
                for dstbase, src_off, k2 in [(0, 0, 1)]:
                    for h in range(H):
                        half, h8 = divmod(h, 8)
                        for t in range(NTILE):
                            bA1 = bb[2 * (item % 2)]
                            bA2 = bb[2 * (item % 2) + 1]
                            slot = 4 + item % 2
                            rbase = ((h8 * NTILE + t) * 3) * 68
                            if first:
                                r.wait("T", *wait_first)
                                first = False
                            r.wait("T", evm.get(bA1))
                            src = src_off + half * 512 + hs * 256
                            r.t(lambda e, bA1=bA1, rbase=rbase, src=src: e.matmul(
                                psv(bA1, 0, 0, [[0, 68], [1, 256]]),
                                rs_sb[:, rbase: rbase + 68],
                                qk_fm[:, src: src + 256],
                                start=True, stop=True))
                            r.t(lambda e, bA2=bA2, rbase=rbase, k2=k2, src=src: e.matmul(
                                psv(bA2, 0, 0, [[0, 68], [1, 256]]),
                                rs_sb[:, rbase + k2 * 68: rbase + (k2 + 1) * 68],
                                qk_fm[:, src: src + 256],
                                start=True, stop=True))
                            ev_ab = r.tick("T")
                            r.wait("S", ev_ab, evm.get(bA1))
                            r.s(lambda e, bA2=bA2, slot=slot: e.activation(
                                epv(slot, 68, 256), psv(bA2, 0, 0, [[0, 68], [1, 256]]),
                                AF.Identity, bias=0.0, scale=1.0))
                            ev_e2 = r.tick("S")
                            r.wait("V", ev_e2)
                            col = dstbase + (t * H + h) * 256
                            r.v(lambda e, col=col, bA1=bA1, slot=slot: e.tensor_tensor(
                                ap(pq, 0, col, [[0, 68], [1, 256]]),
                                psv(bA1, 0, 0, [[0, 68], [1, 256]]),
                                epv(slot, 68, 256),
                                OP.mult))
                            evm[bA1] = r.tick("V")
                            item += 1
                return r.tick("V")

            def restage(hs, bt1, bt2, wait_first):
                ev_done = None
                first = list(wait_first)
                for which, src0, dst0 in [(0, 0, 0), (1, 1024, 4096)]:
                    for tt2 in range(2):
                        tok = hs * 256 + tt2 * 128
                        r.wait("T", *first)
                        first = []
                        for half in range(2):
                            r.t(lambda e, src0=src0, half=half, tok=tok: e.matmul(
                                psv(bt1, 0, half * 128, [[0, 128], [1, 128]]),
                                qk_fm[:, src0 + half * 512 + tok: src0 + half * 512 + tok + 128],
                                ident[:], start=True, stop=True))
                        ev_t1 = r.tick("T")
                        r.wait("V", ev_t1)
                        r.v(lambda e: e.tensor_copy(
                            tmst[:], psv(bt1, 0, 0, [[0, 128], [1, 256]])))
                        ev_tm = r.tick("V")
                        r.wait("T", ev_tm)
                        for grp in range(4):
                            for hh in range(4):
                                h = grp * 4 + hh
                                r.t(lambda e, h=h, hh=hh: e.matmul(
                                    psv(bt2[hh], 0, 0, [[0, 16], [1, 128]]),
                                    tmst[:, h * 16: h * 16 + 16],
                                    ident[:], start=True, stop=True))
                            ev_t2 = r.tick("T")
                            r.wait("V", ev_t2)
                            bstep = (bt2[1] - bt2[0]) * 512
                            r.v(lambda e, grp=grp, dst0=dst0, tt2=tt2, bstep=bstep: e.tensor_copy(
                                bass.AP(qkh, dst0 + grp * 4 * 256 + tt2 * 128,
                                        [[8192, 16], [256, 4], [1, 128]]),
                                psv(bt2[0], 0, 0, [[0, 16], [bstep, 4], [1, 128]])))
                            ev_done = r.tick("V")
                            r.wait("T", ev_done)
                return ev_done

            ev_rst = restage(0, 0, (4, 5, 6, 7), (ev_qk,))
            dbg_cap("d_qkh", lambda: ap(qkh, 0, 0, [[0, 16], [1, 8192]]))
            ev_phi = build_phis(0, (4, 5, 6, 7), (ev_rs, ev_qk, ev_rst))
            dbg_cap("d_phiqa", lambda: ap(pq, 0, 0, [[0, 68], [1, 16 * 256]]))

            ev_as = {}
            ev_sqw = {}
            ev_sadd_z = ev_sadd
            ev_epi_done = None
            ev_szcast = None
            ev_pkm = {}

            def build_phiktm(cidx, buf, prev_mult_ev):
                first_evac = True
                ev_e = None
                for h in range(H):
                    half, h8 = divmod(h, 8)
                    r.t(lambda e, h8=h8, half=half, cidx=cidx: e.matmul(
                        psv(0, 0, 0, [[0, 64], [1, 136]]),
                        qk_fm[:, 1024 + half * 512 + cidx * 64:
                              1024 + half * 512 + cidx * 64 + 64],
                        rm_sb[:, (h8 * 2) * 136: (h8 * 2) * 136 + 136],
                        start=True, stop=True))
                    r.t(lambda e, h8=h8, half=half, cidx=cidx: e.matmul(
                        psv(3, 0, 180, [[0, 64], [1, 136]]),
                        qk_fm[:, 1024 + half * 512 + cidx * 64:
                              1024 + half * 512 + cidx * 64 + 64],
                        rm_sb[:, (h8 * 2 + 1) * 136: (h8 * 2 + 1) * 136 + 136],
                        start=True, stop=True))
                    ev12 = r.tick("T")
                    r.wait("S", ev12)
                    if first_evac:
                        r.wait("S", prev_mult_ev)
                        first_evac = False
                    r.s(lambda e, h=h: e.activation(
                        ap(pk1, 0, h * 136, [[0, 64], [1, 136]]),
                        psv(0, 0, 0, [[0, 64], [1, 136]]),
                        AF.Identity, bias=0.0, scale=1.0))
                    r.s(lambda e, h=h: e.activation(
                        ap(pk2, 0, h * 136, [[0, 64], [1, 136]]),
                        psv(3, 0, 180, [[0, 64], [1, 136]]),
                        AF.Identity, bias=0.0, scale=1.0))
                    ev_e = r.tick("S")
                    r.wait("T", ev_e)
                r.wait("V", ev_e, prev_mult_ev)
                r.v(lambda e, buf=buf: e.tensor_tensor(
                    phiKtm[buf][:], pk1[:], pk2[:], OP.mult))
                return r.tick("V")

            for c in range(NCH):
                gc = gc0 + c
                if c == 0:
                    r.wait("T", ev_phi, ev_sadd_z, prev_ffn_done_V)
                    ev_pkm[0] = build_phiktm(0, 0, ev_pkm.get(-1))
                if c == 4:
                    ev_rst2 = restage(1, 0, (1, 2, 3, 4),
                                      (ev_epi_done, ev_as[1], ev_sadd_z, ev_pkm[c]))
                    ev_phi = build_phis(1, (0, 1, 2, 3), (ev_rst2,))
                if c == 0 and gc0 == 0:
                    dbg_cap("d_phiktm", lambda: phiKtm[0][:])

                if gc > 0:
                    r.wait("V", ev_sadd_z)
                    r.v(lambda e: e.tensor_copy(SZbf[:], Ssb[:]))
                    ev_szcast = r.tick("V")
                    if gc == 1 and dbg and "d_szc" not in dbg_done and "szc" not in _DBG_SKIP:
                        r.v(lambda e: e.tensor_copy(ap(ep, 0, 0, [[0, 68], [1, 65]]),
                                                    ap(SZbf, 0, 0, [[0, 68], [1, 65]])))
                        r.v(lambda e: e.tensor_copy(ap(ep, 0, 65, [[0, 68], [1, 65]]),
                                                    ap(SZbf, 0, 1040, [[0, 68], [1, 65]])))
                        dbg_cap("d_szc", lambda: ap(ep, 0, 0, [[0, 68], [1, 130]]))

                for wave in range(2):
                    r.wait("T", ev_phi, ev_sqw.get(1 - wave), ev_pkm[c])
                    for hh in range(8):
                        h = wave * 8 + hh
                        r.t(lambda e, h=h, c=c, hh=hh: e.matmul(
                            psv(0, 0, hh * 64, [[0, 64], [1, 64]]),
                            ap(qkh, 0, 4096 + h * 256 + (c % 4) * 64, [[0, 16], [1, 64]]),
                            ap(qkh, 0, h * 256 + (c % 4) * 64, [[0, 16], [1, 64]]),
                            start=True, stop=True))
                    ev_pa = r.tick("T")
                    r.wait("S", ev_pa, ev_as.get(wave))
                    r.s(lambda e, wave=wave: e.activation(
                        epv(wave, 64), psv(0, 0, 0, [[0, 64], [1, 512]]),
                        AF.Square, bias=0.0, scale=1.0))
                    ev_sq = r.tick("S")
                    ev_sqw[wave] = ev_sq
                    r.wait("V", ev_sq, ev_mk)
                    r.v(lambda e, wave=wave: e.tensor_tensor(
                        ap(As_sb, 0, wave * 512, [[0, 64], [1, 512]]),
                        epv(wave, 64),
                        ap(maskrep, 0, wave * 512, [[0, 64], [1, 512]]),
                        OP.mult))
                    ev_as[wave] = r.tick("V")
                ev_as_all = ev_as[1]

                po_banks = [(1, 0, 7), (2, 7, 7), (3, 14, 2)]
                r.wait("T", ev_as[0], ev_szcast, ev_epi_done, ev_sadd_z)
                for bank, h0, nh in po_banks:
                    if bank == 2:
                        r.wait("T", ev_as_all)
                    for hh in range(nh):
                        h = h0 + hh
                        first = True
                        if gc > 0:
                            for t in range(NTILE):
                                r.t(lambda e, bank=bank, hh=hh, h=h, t=t, c=c: e.matmul(
                                    psv(bank, 0, hh * 65, [[0, 64], [1, 65]]),
                                    ap(pq, 0, (t * H + h) * 256 + (c % 4) * 64, [[0, 68], [1, 64]]),
                                    ap(SZbf, 0, (t * H + h) * 65, [[0, 68], [1, 65]]),
                                    start=(t == 0), stop=False))
                            first = False
                        r.t(lambda e, bank=bank, hh=hh, h=h, c=c, first=first: e.matmul(
                            psv(bank, 0, hh * 65, [[0, 64], [1, 65]]),
                            ap(As_sb, 0, h * 64, [[0, 64], [1, 64]]),
                            bigv(0, V1C + c * 1040 + h * 65, [[0, 64], [1, 65]]),
                            start=first, stop=True))
                ev_po = r.tick("T")

                r.wait("V", ev_po)
                if c == 0 and gc0 == 0 and dbg and "d_po" not in dbg_done:
                    r.v(lambda e: e.tensor_copy(epv(2, 64, 512), psv(1, 0, 0, [[0, 64], [1, 512]])))
                    dbg_cap("d_po", lambda: epv(2, 64, 512))
                if c == 0 and gc0 == 0 and dbg and "d_ddr" not in dbg_done:
                    pass
                r.wait("S", ev_po)
                for bank, h0, nh in po_banks:
                    r.s(lambda e, bank=bank, h0=h0, nh=nh: e.activation(
                        ap(dd, 0, h0, [[0, 64], [1, nh]]),
                        psv(bank, 0, 64, [[0, 64], [65, nh]]),
                        AF.Identity, bias=EPS, scale=1.0))
                ev_dd = r.tick("S")
                if gc == gc0 + 1 and gc0 == 0 and dbg and "d_poc1" not in dbg_done and "poc1" not in _DBG_SKIP:
                    r.wait("V", ev_po)
                    r.v(lambda e: e.tensor_copy(epv(2, 64, 130), psv(1, 0, 0, [[0, 64], [1, 130]])))
                    dbg_cap("d_poc1", lambda: epv(2, 64, 130))
                if gc == gc0 + 1 and gc0 == 0 and dbg and "d_ddc1" not in dbg_done and "ddc1" not in _DBG_SKIP:
                    r.wait("V", ev_dd)
                    r.v(lambda e: e.tensor_copy(ap(ep, 0, 1536, [[0, 64], [1, 16]]), dd[:]))
                    dbg_cap("d_ddc1", lambda: ap(ep, 0, 1536, [[0, 64], [1, 16]]))
                r.wait("V", ev_dd)
                r.v(lambda e: e.reciprocal(ddr[:], dd[:]))
                r.v(lambda e: e.tensor_copy(stats[:, 9:10], stats[:, 0:1]))
                r.v(lambda e: e.tensor_copy(stats[:, 10:11], stats[:, 0:1]))
                if c == 0 and gc0 == 0 and dbg and "d_ddr" not in dbg_done:
                    dbg_cap("d_ddr", lambda: ddr[:])
                for bank, h0, nh in po_banks:
                    for hh in range(nh):
                        h = h0 + hh
                        r.v(lambda e, bank=bank, hh=hh, h=h, c=c: e.tensor_scalar(
                            bass.AP(big, OTM + c * 1024 + h * 64, [[33024, 64], [1, 64]]),
                            psv(bank, 0, hh * 65, [[0, 64], [1, 64]]),
                            ddr[:, h:h + 1], None, OP.mult))
                ev_epi_done = r.tick("V")
                if c == 0 and gc0 == 0:
                    dbg_cap("d_dd", lambda: ap(dd, 0, 0, [[0, 64], [1, 16]]))
                    dbg_cap("d_as", lambda: ap(As_sb, 0, 0, [[0, 64], [1, 1024]]))

                SD_BANKS = [(4, 0, 0, 7), (5, 7, 0, 7), (6, 14, 0, 7),
                            (7, 21, 0, 7), (3, 28, 140, 4)]
                r.wait("T", ev_pkm[c], ev_sadd_z)
                for bank, s0, coff, ns in SD_BANKS:
                    if bank == 3:
                        r.wait("T", ev_epi_done)
                    for si in range(ns):
                        s = s0 + si
                        t, h = divmod(s, 16)
                        r.t(lambda e, bank=bank, si=si, coff=coff, t=t, h=h, c=c: e.matmul(
                            psv(bank, 0, coff + si * 65, [[0, 68], [1, 65]]),
                            ap(phiKtm[c % 2], 0, h * 136 + t * 68, [[0, 64], [1, 68]]),
                            bigv(0, V1C + c * 1040 + h * 65, [[0, 64], [1, 65]]),
                            start=True, stop=True))
                ev_sd = r.tick("T")

                r.wait("V", ev_sd)
                for bank, s0, coff, ns in SD_BANKS:
                    r.v(lambda e, bank=bank, s0=s0, coff=coff, ns=ns: e.tensor_tensor(
                        bass.AP(Ssb, s0 * 65, [[2080, 68], [1, ns * 65]]),
                        psv(bank, 0, coff, [[0, 68], [1, ns * 65]]),
                        bass.AP(Ssb, s0 * 65, [[2080, 68], [1, ns * 65]]),
                        OP.add))
                ev_sadd_z = r.tick("V")

                if c + 1 < NCH:
                    r.wait("T", ev_sadd_z, ev_pkm[c])
                    ev_pkm[c + 1] = build_phiktm(c + 1, (c + 1) % 2, ev_pkm[c])
            ev_sadd = ev_sadd_z
            if gc0 == 0:
                dbg_cap("d_otm", lambda: bigv(0, OTM, [[0, 64], [1, 8 * 1024]]))
                dbg_cap("d_sz", lambda: ap(Ssb, 0, 0, [[0, 68], [1, 2080]]))

            r.wait("T", ev_epi_done, ev_sadd, ev_id)
            ev_ofm = None
            for k in range(8):
                for c in range(NCH):
                    r.t(lambda e, k=k, c=c: e.matmul(
                        psv(6, 0, c * 64, [[0, 128], [1, 64]]),
                        bass.AP(big, OTM + c * 1024 + k * 128, [[33024, 64], [1, 128]]),
                        ident[0:64, 0:64], start=True, stop=True))
                ev_tr = r.tick("T")
                r.wait("V", ev_tr)
                r.v(lambda e, k=k: e.tensor_copy(
                    o_fm_v(k * 512, 512),
                    psv(6, 0, 0, [[0, 128], [1, 512]])))
                ev_ofm = r.tick("V")
                r.wait("T", ev_ofm)

            ev_w1 = {0: load_w1(0, 0)}
            ev_w1_mm_done = {}
            ev_z0_by_bank = {}
            ev_hb_prev = None
            ev_y2_prev = None
            for fft in range(32):
                bank = fft % 2
                if fft + 1 < 32:
                    if fft >= 1:
                        r.wait("G", ev_w1_mm_done[fft - 1])
                    ev_w1[fft + 1] = load_w1((fft + 1) % 2, fft + 1)
                r.wait("T", ev_w1[fft], ev_z0_by_bank.get(bank), ev_b1s)
                for k in range(8):
                    r.t(lambda e, fft=fft, k=k, bank=bank: e.matmul(
                        psv(bank, 0, 0, [[0, 128], [1, 512]]),
                        w1_sb[fft % 2][:, k * 128: k * 128 + 128],
                        xn_fm[:, k * 512: (k + 1) * 512],
                        start=(k == 0), stop=(k == 7)))
                ev_h = r.tick("T")
                ev_w1_mm_done[fft] = ev_h
                r.wait("S", ev_h, ev_hb_prev, ev_y2_prev)
                r.s(lambda e, bank=bank, fft=fft: e.activation(
                    epv(0), psv(bank, 0, 0, [[0, 128], [1, 512]]),
                    AF.Identity, bias=b1s[:, fft:fft + 1], scale=256.0))
                ev_z0 = r.tick("S")
                ev_z0_by_bank[bank] = ev_z0
                r.wait("V", ev_z0, ev_y2_prev)
                r.v(lambda e: e.tensor_scalar(epv(1), epv(0), MAGIC,
                                              MAGIC - 1024.0, OP.add, OP.max))
                r.v(lambda e: e.tensor_scalar(epv(2), epv(1), MAGIC + 1024.0,
                                              -(MAGIC - 1024.0), OP.min, OP.add))
                ev_vp = r.tick("V")
                r.wait("S", ev_vp)
                r.s(lambda e: e.activation(epv(3), epv(2), AF.Identity,
                                           bias=0.0, scale=1.0 / 64.0))
                r.s(lambda e: e.activation(epi[:], epv(2), AF.Identity,
                                           bias=0.0, scale=1.0 / 64.0))
                ev_i0 = r.tick("S")
                r.wait("V", ev_i0)
                r.v(lambda e: e.tensor_tensor(epv(4), epi[:], epv(3), OP.is_gt))
                r.v(lambda e: e.tensor_tensor(epv(2), epi[:], epv(4), OP.subtract))
                r.v(lambda e: e.tensor_tensor(epv(3), epv(3), epv(2), OP.subtract))
                ev_it = r.tick("V")
                r.wait("S", ev_it)
                r.s(lambda e: e.activation(epv(4), epv(2), AF.Gelu,
                                           bias=-4.0, scale=0.25))
                r.s(lambda e: e.activation(epv(5), epv(2), AF.Gelu,
                                           bias=-3.75, scale=0.25))
                ev_g = r.tick("S")
                r.wait("V", ev_g)
                r.v(lambda e: e.tensor_tensor(epv(5), epv(5), epv(4), OP.subtract))
                r.v(lambda e: e.tensor_tensor(epv(5), epv(3), epv(5), OP.mult))
                r.v(lambda e: e.tensor_tensor(epv(4), epv(5), epv(4), OP.add))
                r.v(lambda e: e.tensor_scalar(epi[:], epv(1), MAGIC + 1024.0,
                                              None, OP.is_gt))
                r.v(lambda e: e.tensor_scalar(epv(2), epv(1), -MAGIC,
                                              1.0 / 256.0, OP.add, OP.mult))
                r.v(lambda e: e.select(epv(3), epi[:], epv(2), epv(4)))
                r.v(lambda e: e.tensor_scalar(epv(5), epv(0), -1024.5,
                                              None, OP.is_ge))
                r.v(lambda e: e.tensor_tensor(epv(4), epv(3), epv(5), OP.mult))
                ev_y2 = r.tick("V")
                ev_y2_prev = ev_y2
                r.wait("S", ev_y2)
                r.s(lambda e: e.activation(epv(5), epv(4), AF.Identity,
                                           bias=MAGIC + 1024.0, scale=256.0))
                r.s(lambda e, fft=fft: e.activation(
                    bass.AP(big, H_COLS + fft * 512, [[33024, 128], [1, 512]]),
                    epv(5), AF.Identity, bias=-49156.0, scale=1.0 / 256.0))
                ev_hb_prev = r.tick("S")
                if fft == 0 and gc0 == 0 and b == 0:
                    dbg_cap("d_h", lambda: epv(4))
            ev_h_all = ev_hb_prev

            ev_w2 = {(0, 0): load_w2(0, 0, 0)}
            r.wait("G", ev_sadd)
            ev_wo_dh = {0: load_wo(0)}
            ev_out = None
            ev_z4_prev = None
            prev_store_any = None
            ev_mm2_read = {}
            ev_mm2_sl_prev = None
            for dh in range(2):
                for sl in range(16):
                    nxt = (dh, sl + 1) if sl + 1 < 16 else ((1, 0) if dh == 0 else None)
                    if nxt is not None:
                        if (dh, sl) != (0, 0):
                            r.wait("G", ev_mm2_sl_prev)
                        ev_w2[nxt] = load_w2((sl + 1 + dh * 16) % 2, nxt[0], nxt[1])
                    r.wait("T", ev_w2[(dh, sl)],
                           ev_hb_prev if (dh == 0 and sl == 0) else None,
                           ev_mm2_read.get(dh - 1))
                    for ffcl in range(2):
                        ffc = sl * 2 + ffcl
                        for tt in range(NTT):
                            r.t(lambda e, dh=dh, sl=sl, ffcl=ffcl, ffc=ffc, tt=tt: e.matmul(
                                psv(2 + tt, 0, 0, [[0, 128], [1, 512]]),
                                bass.AP(big, H_COLS + ffc * 512 + tt * 128,
                                        [[33024, 128], [1, 128]]),
                                w2_sb[(sl + dh * 16) % 2][:, ffcl * 512: ffcl * 512 + 512],
                                start=(sl == 0 and ffcl == 0), stop=(sl == 15 and ffcl == 1)))
                    ev_mm2_sl_prev = r.tick("T")
                ev_mm2 = ev_mm2_sl_prev
                for tt in range(NTT):
                    row = tok0 + tt * 128
                    r.wait("Q", ev_y2_prev, ev_out)
                    evxr = r.q(lambda e, row=row: e.dma_start(
                        xr_v(), bass.AP(x_d.tensor, row * D, [[D, 128], [1, D]])), "qx")
                    r.wait("V", ev_mm2, ev_b2t, prev_store_any, ev_z4_prev)
                    r.v(lambda e, tt=tt, dh=dh: e.scalar_tensor_tensor(
                        osb[0][:], psv(2 + tt, 0, 0, [[0, 128], [1, 512]]),
                        256.0, b2t[:, dh * 512: dh * 512 + 512], OP.mult, OP.add))
                    ev_z3 = r.tick("V")
                    r.wait("S", ev_z3, ev_out)
                    r.s(lambda e: e.activation(u4_v(), osb[0][:], AF.Identity,
                                               bias=MAGIC, scale=1.0))
                    ev_z4_prev = r.tick("S")
                    r.s(lambda e: e.activation(q3_v(), u4_v(), AF.Identity,
                                               bias=-49152.0, scale=1.0 / 256.0))
                    ev_q3 = r.tick("S")
                    r.wait("T", ev_z3, ev_ofm, ev_wo_dh[dh])
                    for k in range(8):
                        r.t(lambda e, k=k, tt=tt: e.matmul(
                            psv(2 + tt, 0, 0, [[0, 128], [1, 512]]),
                            o_fm_v(k * 512 + tt * 128, 128),
                            ap(pq, 0, 8192 + k * 512, [[0, 128], [1, 512]]),
                            start=(k == 0), stop=(k == 7)))
                    ev_wom = r.tick("T")
                    if tt == NTT - 1 and dh == 0:
                        r.wait("G", ev_wom)
                        ev_wo_dh[1] = load_wo(1)
                    r.wait("V", ev_wom, evxr, ev_q3)
                    r.v(lambda e, dh=dh, tt=tt: e.tensor_tensor(
                        osb[1][:], xr_h(dh),
                        psv(2 + tt, 0, 0, [[0, 128], [1, 512]]), OP.add))
                    r.v(lambda e: e.tensor_tensor(osb[1][:], osb[1][:], q3_v(), OP.add))
                    ev_out = r.tick("V")
                    r.wait("Q", ev_out)
                    ev_st = r.q(lambda e, row=row, dh=dh: e.dma_start(
                        bass.AP(out_d.tensor, row * D + dh * 512, [[D, 128], [1, 512]]),
                        osb[1][:]), "qout")
                    prev_store_any = ev_st
                    last_out_ev = ev_st
                ev_mm2_read[dh] = ev_out
            prev_ffn_done_V = ev_out
            prev_ffn_done_S = ev_q3

    r.final_waits("Q")
    sems = {}
    for e in list("TVS") + sorted(r.streams):
        sems[e] = es.enter_context(nc.semaphore(f"sem{e}"))
    with es:
        r.replay(nc, sems)
    return nc


def kernel(L0_attn_hidden_states, L0_attn_Wq, L0_attn_Wk, L0_attn_Wv, L0_attn_Wo,
           L0_ffn_w1, L0_ffn_b1, L0_ffn_w2, L0_ffn_b2):
    x = np.ascontiguousarray(np.asarray(L0_attn_hidden_states, np.float32))
    consts = host_consts()
    shared = dict(
        wq=np.ascontiguousarray(L0_attn_Wq, np.float32),
        wk=np.ascontiguousarray(L0_attn_Wk, np.float32),
        wv=np.ascontiguousarray(L0_attn_Wv, np.float32),
        wo=np.ascontiguousarray(L0_attn_Wo, np.float32),
        w1=np.ascontiguousarray(L0_ffn_w1, np.float32),
        b1=np.ascontiguousarray(L0_ffn_b1, np.float32),
        w2=np.ascontiguousarray(L0_ffn_w2, np.float32),
        b2=np.ascontiguousarray(L0_ffn_b2, np.float32),
        **consts)
    in_maps = []
    for c in range(NCORES):
        m = dict(shared)
        m["x"] = np.ascontiguousarray(
            x[c * B_PER_CORE:(c + 1) * B_PER_CORE].reshape(B_PER_CORE * L, D))
        in_maps.append(m)
    nc = build_nc(dbg=_DEBUG)
    res = run_bass_kernel_spmd(nc, in_maps, core_ids=list(range(NCORES)))
    out = np.concatenate([r["out"].reshape(B_PER_CORE, L, D) for r in res.results],
                         axis=0)
    if _DEBUG:
        kernel._dbg = list(res.results)
    return out


# version 08 (final)
# speedup vs baseline: 1.0222x; 1.0129x over previous
import contextlib
import numpy as np
import ml_dtypes
import concourse.bass as bass
import concourse.mybir as mybir
from concourse.bass_utils import run_bass_kernel_spmd

F32 = mybir.dt.float32
BF16 = mybir.dt.bfloat16
I32 = mybir.dt.int32
AF = mybir.ActivationFunctionType
OP = mybir.AluOpType
AX = mybir.AxisListType

MAGIC = 12582912.0
EPS = 1e-5
NCORES = 8
B_PER_CORE = 2
L = 4096
D = 1024
FF = 4096
H = 16
CH = 64
SEG = 512
NSEG = L // SEG
NCH = SEG // CH
NTT = SEG // 128

PAIRS = [(i, j) for i in range(16) for j in range(i, 16)]
TA, TB = PAIRS[:68], PAIRS[68:]
NTILE = 2

_DEBUG = False
_DBG_SKIP = frozenset()

CONSTS = [MAGIC, MAGIC + 1024.0, MAGIC - 1024.0, -4.0, -3.75, 256.0,
          1.0 / 64.0, 1.0 / 256.0, 0.25, -49152.0, -49156.0, EPS,
          -1024.5, -(MAGIC - 1024.0)]


def _bf(x):
    return np.asarray(x, dtype=np.float32).astype(ml_dtypes.bfloat16)


def host_consts():
    d = {}
    d["cpool"] = np.tile(np.array(CONSTS, np.float32), (128, 1))
    rs = np.zeros((128, 8 * NTILE * 3 * 68), np.float32)
    for h8 in range(8):
        for t in range(NTILE):
            pairs = (TA, TB)[t]
            for kind in range(3):
                base = ((h8 * NTILE + t) * 3 + kind) * 68
                for pp, (i, j) in enumerate(pairs):
                    if kind == 0:
                        rs[h8 * 16 + i, base + pp] = 1.0
                    elif kind == 1:
                        rs[h8 * 16 + j, base + pp] = 1.0
                    else:
                        rs[h8 * 16 + j, base + pp] = 2.0 if i != j else 1.0
    d["rs"] = _bf(rs)
    rm = np.zeros((128, 8 * 2 * 136), np.float32)
    for h8 in range(8):
        for kind in range(2):
            base = (h8 * 2 + kind) * 136
            for p, (i, j) in enumerate(PAIRS):
                if kind == 0:
                    rm[h8 * 16 + i, base + p] = 1.0
                else:
                    rm[h8 * 16 + j, base + p] = 2.0 if i != j else 1.0
    d["rm"] = _bf(rm)
    m = np.zeros((64, H * 64), np.float32)
    tril = (np.arange(64)[:, None] <= np.arange(64)[None, :]).astype(np.float32)
    for h in range(H):
        m[:, h * 64:(h + 1) * 64] = tril
    d["maskrep"] = _bf(m)
    d["ident"] = _bf(np.eye(128, dtype=np.float32))
    return d


class Rec:
    ENG = "TVSQG"

    def __init__(self):
        self.ops = {e: [] for e in self.ENG}
        self.cnt = {}
        self.waited = {}
        self.streams = set()

    def emit(self, e, fn):
        self.ops[e].append(("i", [fn, 0, e]))
        return self

    def t(self, fn):
        return self.emit("T", fn)

    def v(self, fn):
        return self.emit("V", fn)

    def s(self, fn):
        return self.emit("S", fn)

    def q(self, fn, stream):
        self.streams.add(stream)
        self.ops["Q"].append(("i", [fn, 16, stream]))
        self.cnt[stream] = self.cnt.get(stream, 0) + 16
        return (stream, self.cnt[stream])

    def g(self, fn, stream):
        self.streams.add(stream)
        self.ops["G"].append(("i", [fn, 16, stream]))
        self.cnt[stream] = self.cnt.get(stream, 0) + 16
        return (stream, self.cnt[stream])

    def tick(self, e):
        for kind, ent in reversed(self.ops[e]):
            if kind == "i":
                if ent[1] == 0:
                    ent[1] = 1
                    self.cnt[e] = self.cnt.get(e, 0) + 1
                return (e, self.cnt[e])
        raise RuntimeError(f"no instruction on {e}")

    def final_waits(self, e):
        for st in sorted(self.streams):
            self.ops[e].append(("w", (st, self.cnt.get(st, 0))))

    def wait(self, e, *evs):
        for ev in evs:
            if ev is None:
                continue
            pe, v = ev
            if pe == e:
                continue
            k = (e, pe)
            if self.waited.get(k, 0) >= v:
                continue
            self.waited[k] = v
            self.ops[e].append(("w", (pe, v)))

    def replay(self, nc, sems):
        def run_engine(e, eng):
            for kind, payload in self.ops[e]:
                if kind == "w":
                    pe, v = payload
                    eng.wait_ge(sems[pe], v)
                else:
                    fn, inc, sem = payload
                    ins = fn(eng)
                    if inc:
                        ins.then_inc(sems[sem], inc)

        with nc.Block() as block:
            @block.tensor
            def _(eng):
                run_engine("T", eng)

            @block.vector
            def _(eng):
                run_engine("V", eng)

            @block.scalar
            def _(eng):
                run_engine("S", eng)

            @block.sync
            def _(eng):
                run_engine("Q", eng)

            @block.gpsimd
            def _(eng):
                run_engine("G", eng)


def ap(t, poff, coff, dims):
    rw = t.shape[1]
    return bass.AP(t, poff * rw + coff, [[rw, dims[0][1]]] + list(dims[1:]))


def build_nc(dbg=False):
    nc = bass.Bass("TRN2", target_bir_lowering=False,
                   detect_race_conditions=False)
    x_d = nc.dram_tensor("x", [B_PER_CORE * L, D], F32, kind="ExternalInput").ap()
    wq_d = nc.dram_tensor("wq", [D, 256], F32, kind="ExternalInput").ap()
    wk_d = nc.dram_tensor("wk", [D, 256], F32, kind="ExternalInput").ap()
    wv_d = nc.dram_tensor("wv", [D, D], F32, kind="ExternalInput").ap()
    wo_d = nc.dram_tensor("wo", [D, D], F32, kind="ExternalInput").ap()
    w1_d = nc.dram_tensor("w1", [D, FF], F32, kind="ExternalInput").ap()
    b1_d = nc.dram_tensor("b1", [FF], F32, kind="ExternalInput").ap()
    w2_d = nc.dram_tensor("w2", [FF, D], F32, kind="ExternalInput").ap()
    b2_d = nc.dram_tensor("b2", [D], F32, kind="ExternalInput").ap()
    cp_d = nc.dram_tensor("cpool", [128, len(CONSTS)], F32, kind="ExternalInput").ap()
    rs_d = nc.dram_tensor("rs", [128, 8 * NTILE * 3 * 68], BF16, kind="ExternalInput").ap()
    rm_d = nc.dram_tensor("rm", [128, 8 * 2 * 136], BF16, kind="ExternalInput").ap()
    mk_d = nc.dram_tensor("maskrep", [64, H * 64], BF16, kind="ExternalInput").ap()
    id_d = nc.dram_tensor("ident", [128, 128], BF16, kind="ExternalInput").ap()
    out_d = nc.dram_tensor("out", [B_PER_CORE * L, D], F32, kind="ExternalOutput").ap()
    dbg_d = {}
    if dbg:
        for name, shape in [("d_xn", [128, D]), ("d_q", [128, 1024]),
                            ("d_k", [128, 1024]), ("d_v1", [64, 8 * 1040]),
                            ("d_phiqa", [68, 16 * 256]), ("d_phikfa", [68, 16 * 256]),
                            ("d_phiktm", [64, 16 * 136]), ("d_otm", [64, 8 * 1024]),
                            ("d_h", [128, 512]), ("d_sz", [68, 2080]),
                            ("d_dd", [64, 16]), ("d_as", [64, 1024]),
                            ("d_qkh", [16, 8192]), ("d_po", [64, 512]), ("d_ddr", [64, 16]),
                            ("d_szc", [68, 130]), ("d_poc1", [64, 130]), ("d_ddc1", [64, 16]),
                            ("d_ln0", [128, 12])]:
            dbg_d[name] = nc.dram_tensor(name, shape, F32, kind="ExternalOutput").ap()

    es = contextlib.ExitStack()
    sb = lambda name, shape, dt=F32: es.enter_context(nc.sbuf_tensor(name, shape, dt))

    cpool = sb("cpool_sb", [128, len(CONSTS)])
    wq_sb = sb("wq_sb", [128, 2048], BF16)
    wk_sb = sb("wk_sb", [128, 2048], BF16)
    w1_sb = [sb(f"w1_sb{i}", [128, 1024], BF16) for i in range(2)]
    w2_sb = [sb(f"w2_sb{i}", [128, 1024], BF16) for i in range(2)]
    b1s = sb("b1s", [128, 32])
    b2t = sb("b2t", [128, D], BF16)
    rs_sb = sb("rs_sb", [128, 8 * NTILE * 3 * 68], BF16)
    rm_sb = sb("rm_sb", [128, 8 * 2 * 136], BF16)
    maskrep = sb("maskrep_sb", [64, H * 64], BF16)
    ident = sb("ident_sb", [128, 128], BF16)
    xnt = sb("xnt", [128, D], BF16)
    stats = sb("stats", [128, 12])
    mv = sb("mv", [128, 2])
    negm = sb("negm", [128, 1])
    rstd = sb("rstd", [128, 1])
    osb = [sb(f"osb{i}", [128, 512]) for i in range(2)]
    xn_fm = sb("xn_fm", [128, 8 * SEG], BF16)
    qk_fm = sb("qk_fm", [128, 2 * 1024], BF16)
    pq = sb("pq", [128, 16384], BF16)
    qkh = sb("qkh", [16, 8192], BF16)
    tmst = sb("tmst", [128, 256], BF16)
    phiKtm_t = sb("phiKtm", [64, H * 136], BF16)
    phiKtm = [phiKtm_t, phiKtm_t]
    pk1 = sb("pk1", [64, H * 136], BF16)
    pk2 = sb("pk2", [64, H * 136], BF16)
    As_sb = sb("As_sb", [64, H * 64], BF16)
    Ssb = sb("Ssb", [68, NTILE * H * 65])
    SZbf = sb("SZbf", [68, NTILE * H * 65], BF16)
    big = sb("big", [128, 33024], BF16)
    dd = sb("dd", [64, 16])
    ddr = sb("ddr", [64, 16])
    ep = sb("ep", [128, 6 * 512])
    epi = sb("epi", [128, 512], I32)
    u4 = None
    q3v = None

    PS = es.enter_context(nc.psum_tensor("PS", [128, 4096], F32))

    H_COLS = 0
    OTM = 16384
    V1C = 24576

    def bigv(poff, coff, dims):
        return ap(big, poff, coff, dims)

    def o_fm_v(c0, w):
        return ap(big, 0, V1C + c0, [[0, 128], [1, w]])

    def psv(bank, poff, coff, dims):
        return ap(PS, poff, bank * 512 + coff, dims)

    def epv(i, rows=128, w=512):
        return ap(ep, 0, i * 512, [[0, rows], [1, w]])

    xr_v = lambda: ap(ep, 0, 1024, [[0, 128], [1, 1024]])
    xt_v = xr_v
    xr_h = lambda dh: ap(ep, 0, 1024 + dh * 512, [[0, 128], [1, 512]])
    u4_v = lambda: epv(0)
    q3_v = lambda: epv(1)

    for i, val in enumerate(CONSTS):
        nc.const_aps.register(F32, val, cpool[:, i:i + 1])

    r = Rec()

    r.q(lambda e: e.dma_start(cpool[:], cp_d), "qset")
    r.q(lambda e: e.dma_start(rs_sb[:], rs_d), "qset")
    r.q(lambda e: e.dma_start(rm_sb[:], rm_d), "qset")
    r.q(lambda e: e.dma_start(maskrep[:], mk_d), "qset")
    r.q(lambda e: e.dma_start(ident[:], id_d), "qset")
    r.q(lambda e: e.dma_start(ap(ep, 0, 0, [[0, 128], [1, 32]]),
                              bass.AP(b1_d.tensor, 0, [[1, 128], [128, 32]]),
                              allow_slow_non_contiguous=True), "qset")
    ev_setup_q = r.q(lambda e: e.dma_start(ap(ep, 0, 2048, [[0, 1], [1, D]]),
                                           bass.AP(b2_d.tensor, 0, [[D, 1], [1, D]])), "qset")
    r.wait("V", ev_setup_q)
    r.v(lambda e: e.tensor_scalar(b1s[:], ap(ep, 0, 0, [[0, 128], [1, 32]]),
                                  256.0, None, OP.mult))
    ev_b1s = r.tick("V")
    r.v(lambda e: e.tensor_scalar(ap(big, 0, 0, [[0, 1], [1, D]]),
                                  ap(ep, 0, 2048, [[0, 1], [1, D]]),
                                  256.0, None, OP.mult))
    r.v(lambda e: e.memset(ap(big, 0, 1024, [[0, 1], [1, 128]]), 1.0))
    ev_b2c = r.tick("V")
    r.wait("T", ev_b2c)
    for jj in range(2):
        r.t(lambda e, jj=jj: e.matmul(
            psv(jj, 0, 0, [[0, 128], [1, 512]]),
            ap(big, 0, 1024, [[0, 1], [1, 128]]),
            ap(big, 0, jj * 512, [[0, 1], [1, 512]]),
            start=True, stop=True))
    ev_b2m = r.tick("T")
    r.wait("V", ev_b2m)
    for jj in range(2):
        r.v(lambda e, jj=jj: e.tensor_copy(b2t[:, jj * 512:(jj + 1) * 512],
                                           psv(jj, 0, 0, [[0, 128], [1, 512]])))
    ev_b2t = r.tick("V")
    ev_wq = r.g(lambda e: e.dma_start(
        wq_sb[:], bass.AP(wq_d.tensor, 0, [[256, 128], [128 * 256, 8], [1, 256]])), "gwq")
    ev_wk = r.g(lambda e: e.dma_start(
        wk_sb[:], bass.AP(wk_d.tensor, 0, [[256, 128], [128 * 256, 8], [1, 256]])), "gwk")
    ev_id = ev_mk = ev_rs = ev_setup_q
    setup_evs = [ev_setup_q, ev_b1s, ev_b2t, ev_wq, ev_wk]

    def load_w1(buf, fft):
        return r.g(lambda e: e.dma_start(
            w1_sb[buf][:], bass.AP(w1_d.tensor, fft * 128,
                                   [[FF, 128], [128 * FF, 8], [1, 128]])), f"gw1{buf}")

    def load_w2(buf, dh, sl):
        return r.g(lambda e: e.dma_start(
            w2_sb[buf][:], bass.AP(w2_d.tensor, (sl * 256) * D + dh * 512,
                                   [[D, 128], [128 * D, 2], [1, 512]])), f"gw2{buf}")

    def load_wo(dh):
        return r.g(lambda e: e.dma_start(
            ap(pq, 0, 8192, [[0, 128], [512, 8], [1, 512]]),
            bass.AP(wo_d.tensor, dh * 512, [[D, 128], [128 * D, 8], [1, 512]])), "gwo")

    def load_wv():
        return r.g(lambda e: e.dma_start(
            ap(pq, 0, 8192, [[0, 128], [1024, 8], [1, 1024]]),
            bass.AP(wv_d.tensor, 0, [[D, 128], [128 * D, 8], [1, D]])), "gwv")

    dbg_done = set()

    def dbg_cap(name, src_ap_fn, prod="V"):
        if dbg and name not in dbg_done:
            dbg_done.add(name)
            ev_ = r.tick(prod)
            r.wait("G", ev_)
            evc = r.g(lambda e: e.dma_start(dbg_d[name], src_ap_fn()), "gdbg")
            for eng in "TVS":
                r.wait(eng, evc)

    last_out_ev = None
    prev_ffn_done_V = None
    prev_ffn_done_S = None
    ev_xm = None
    for b in range(B_PER_CORE):
        r.wait("V", *setup_evs)
        if b == 0:
            r.v(lambda e: e.memset(stats[:], 0.0))
        r.v(lambda e: e.memset(Ssb[:], 0.0))
        ev_sadd = r.tick("V")

        for seg in range(NSEG):
            tok0 = b * L + seg * SEG
            gc0 = seg * NCH

            ev_tr_last = None
            for tt in range(NTT):
                row = tok0 + tt * 128
                r.wait("Q", prev_ffn_done_V)
                evx = r.q(lambda e, row=row: e.dma_start(
                    xt_v(), bass.AP(x_d.tensor, row * D, [[D, 128], [1, D]])), "qx")
                r.wait("S", evx, prev_ffn_done_S, prev_ffn_done_V)
                r.s(lambda e: e.activation(epv(0, 128, 1024), xt_v(), AF.Square,
                                           bias=0.0, scale=1.0 / 32.0))
                ev_sq2 = r.tick("S")
                r.wait("V", evx, prev_ffn_done_V, ev_xm, ev_sq2)
                r.v(lambda e: e.tensor_reduce(stats[:, 2:3], xt_v(), AX.X, OP.add))
                r.v(lambda e: e.tensor_reduce(stats[:, 3:4], epv(0, 128, 1024),
                                              AX.X, OP.add))
                r.v(lambda e: e.tensor_copy(epv(0, 128, 512), xt_v()[:, 0:512]))
                r.v(lambda e: e.tensor_copy(ap(ep, 0, 512, [[0, 128], [1, 512]]),
                                            xt_v()[:, 512:1024]))
                r.v(lambda e: e.tensor_scalar(negm[:], stats[:, 2:3],
                                              -1.0 / 1024.0, None, OP.mult))
                r.v(lambda e: e.tensor_copy(epv(0, 128, 256), xt_v()[:, 0:256]))
                r.v(lambda e: e.tensor_tensor(mv[:, 0:1], negm[:], negm[:], OP.mult))
                r.v(lambda e: e.tensor_copy(ap(ep, 0, 256, [[0, 128], [1, 256]]),
                                            xt_v()[:, 256:512]))
                r.v(lambda e: e.scalar_tensor_tensor(stats[:, 0:1], stats[:, 3:4],
                                                     1.0, mv[:, 0:1],
                                                     OP.mult, OP.subtract))
                ev_var = r.tick("V")
                if dbg and "d_ln0" not in dbg_done and "ln0" not in _DBG_SKIP:
                    r.v(lambda e: e.tensor_copy(ap(ep, 0, 2048, [[0, 128], [1, 4]]),
                                                xt_v()[:, 0:4]))
                    r.v(lambda e: e.tensor_copy(ap(ep, 0, 2052, [[0, 128], [1, 4]]),
                                                stats[:, 0:4]))
                    r.v(lambda e: e.tensor_copy(ap(ep, 0, 2056, [[0, 128], [1, 1]]),
                                                negm[:]))
                    r.v(lambda e: e.tensor_copy(ap(ep, 0, 2057, [[0, 128], [1, 1]]),
                                                mv[:, 0:1]))
                    r.v(lambda e: e.tensor_copy(ap(ep, 0, 2058, [[0, 128], [1, 1]]),
                                                stats[:, 9:10]))
                    r.v(lambda e: e.tensor_copy(ap(ep, 0, 2059, [[0, 128], [1, 1]]),
                                                stats[:, 0:1]))
                    dbg_cap("d_ln0", lambda: ap(ep, 0, 2048, [[0, 128], [1, 12]]))
                r.wait("S", ev_var)
                r.s(lambda e: e.activation(rstd[:], stats[:, 0:1], AF.Sqrt,
                                           bias=EPS, scale=1.0))
                ev_sqr = r.tick("S")
                r.wait("V", ev_sqr)
                r.v(lambda e: e.reciprocal(stats[:, 1:2], rstd[:]))
                r.v(lambda e: e.tensor_copy(stats[:, 9:10], stats[:, 0:1]))
                r.v(lambda e: e.tensor_copy(stats[:, 10:11], stats[:, 0:1]))
                ev_rstd = r.tick("V")
                r.wait("S", ev_rstd)
                r.s(lambda e: e.activation(epv(0, 128, 1024), xt_v(), AF.Identity,
                                           bias=negm[:], scale=1.0))
                ev_xm = r.tick("S")
                r.wait("V", ev_xm, ev_tr_last)
                r.v(lambda e: e.tensor_scalar(xnt[:], epv(0, 128, 1024),
                                              stats[:, 1:2], None, OP.mult))
                ev_xnt = r.tick("V")
                if tt == 0:
                    dbg_cap("d_xn", lambda: xnt[:])
                r.wait("T", ev_xnt, ev_id)
                for k in range(8):
                    r.t(lambda e, k=k: e.matmul(
                        psv(0, 0, 0, [[0, 128], [1, 128]]),
                        xnt[:, k * 128:(k + 1) * 128], ident[:],
                        start=True, stop=True))
                    ev_tr = r.tick("T")
                    r.wait("V", ev_tr)
                    r.v(lambda e, k=k, tt=tt: e.tensor_copy(
                        xn_fm[:, k * 512 + tt * 128: k * 512 + tt * 128 + 128],
                        psv(0, 0, 0, [[0, 128], [1, 128]])))
                    ev_ev = r.tick("V")
                    r.wait("T", ev_ev)
                    ev_tr_last = ev_tr
                r.wait("Q", ev_xm)
            ev_xn_all = r.tick("V")

            r.wait("G", prev_ffn_done_V,
                   ev_xn_all if (b == 0 and seg == 0) else None)
            ev_wv = load_wv()
            r.wait("T", ev_xn_all, ev_wq, ev_wk)
            ev_qk = None
            for which, (wsb, dst0, scale) in enumerate(
                    [(wq_sb, 0, 0.25), (wk_sb, 1024, 1.0)]):
                for half in range(2):
                    for k in range(8):
                        r.t(lambda e, wsb=wsb, half=half, k=k: e.matmul(
                            psv(half, 0, 0, [[0, 128], [1, 512]]),
                            wsb[:, k * 256 + half * 128: k * 256 + half * 128 + 128],
                            xn_fm[:, k * 512: (k + 1) * 512],
                            start=(k == 0), stop=(k == 7)))
                    ev_mm = r.tick("T")
                    r.wait("S", ev_mm)
                    r.s(lambda e, dst0=dst0, half=half, scale=scale: e.activation(
                        qk_fm[:, dst0 + half * 512: dst0 + half * 512 + 512],
                        psv(half, 0, 0, [[0, 128], [1, 512]]),
                        AF.Identity, bias=0.0, scale=scale))
                    ev_qk = r.tick("S")
                    r.wait("T", ev_qk)
            if dbg:
                r.wait("V", ev_qk)
                r.v(lambda e: e.tensor_copy(epv(0, 128, 1024), qk_fm[:, 0:1024]))
                dbg_cap("d_q", lambda: epv(0, 128, 1024))
                r.v(lambda e: e.tensor_copy(epv(2, 128, 1024), qk_fm[:, 1024:2048]))
                ev_dk = r.tick("V")
                dbg_cap("d_k", lambda: epv(2, 128, 1024))

            r.wait("V", prev_ffn_done_V)
            r.v(lambda e: e.memset(bigv(0, V1C + 64, [[0, 64], [1040, 8], [65, 16]]), 1.0))
            ev_ones = r.tick("V")
            r.wait("T", ev_wv, ev_ones, prev_ffn_done_V)
            for c in range(NCH):
                for j in range(2):
                    for k in range(8):
                        r.t(lambda e, c=c, j=j, k=k: e.matmul(
                            psv(2 + j, 0, 0, [[0, 64], [1, 512]]),
                            xn_fm[:, k * 512 + c * 64: k * 512 + c * 64 + 64],
                            ap(pq, 0, 8192 + k * 1024 + j * 512, [[0, 128], [1, 512]]),
                            start=(k == 0), stop=(k == 7)))
                ev_vmm = r.tick("T")
                r.wait("V", ev_vmm)
                for j in range(2):
                    r.v(lambda e, c=c, j=j: e.tensor_copy(
                        bigv(0, V1C + c * 1040 + j * 8 * 65, [[0, 64], [65, 8], [1, 64]]),
                        psv(2 + j, 0, 0, [[0, 64], [1, 512]])))
                ev_v1 = r.tick("V")
                r.wait("T", ev_v1)
            dbg_cap("d_v1", lambda: bigv(0, V1C, [[0, 64], [1, 8 * 1040]]))

            def build_phis(hs, banks, wait_first):
                bb = banks
                evm = {}
                item = 0
                first = True
                for dstbase, src_off, k2 in [(0, 0, 1)]:
                    for h in range(H):
                        half, h8 = divmod(h, 8)
                        for t in range(NTILE):
                            bA1 = bb[2 * (item % 2)]
                            bA2 = bb[2 * (item % 2) + 1]
                            slot = 4 + item % 2
                            rbase = ((h8 * NTILE + t) * 3) * 68
                            if first:
                                r.wait("T", *wait_first)
                                first = False
                            r.wait("T", evm.get(bA1))
                            src = src_off + half * 512 + hs * 256
                            r.t(lambda e, bA1=bA1, rbase=rbase, src=src: e.matmul(
                                psv(bA1, 0, 0, [[0, 68], [1, 256]]),
                                rs_sb[:, rbase: rbase + 68],
                                qk_fm[:, src: src + 256],
                                start=True, stop=True))
                            r.t(lambda e, bA2=bA2, rbase=rbase, k2=k2, src=src: e.matmul(
                                psv(bA2, 0, 0, [[0, 68], [1, 256]]),
                                rs_sb[:, rbase + k2 * 68: rbase + (k2 + 1) * 68],
                                qk_fm[:, src: src + 256],
                                start=True, stop=True))
                            ev_ab = r.tick("T")
                            r.wait("S", ev_ab, evm.get(bA1))
                            r.s(lambda e, bA2=bA2, slot=slot: e.activation(
                                epv(slot, 68, 256), psv(bA2, 0, 0, [[0, 68], [1, 256]]),
                                AF.Identity, bias=0.0, scale=1.0))
                            ev_e2 = r.tick("S")
                            r.wait("V", ev_e2)
                            col = dstbase + (t * H + h) * 256
                            r.v(lambda e, col=col, bA1=bA1, slot=slot: e.tensor_tensor(
                                ap(pq, 0, col, [[0, 68], [1, 256]]),
                                psv(bA1, 0, 0, [[0, 68], [1, 256]]),
                                epv(slot, 68, 256),
                                OP.mult))
                            evm[bA1] = r.tick("V")
                            item += 1
                return r.tick("V")

            def restage(hs, bt1, bt2, wait_first):
                ev_done = None
                first = list(wait_first)
                for which, src0, dst0 in [(0, 0, 0), (1, 1024, 4096)]:
                    for tt2 in range(2):
                        tok = hs * 256 + tt2 * 128
                        r.wait("T", *first)
                        first = []
                        for half in range(2):
                            r.t(lambda e, src0=src0, half=half, tok=tok: e.matmul(
                                psv(bt1, 0, half * 128, [[0, 128], [1, 128]]),
                                qk_fm[:, src0 + half * 512 + tok: src0 + half * 512 + tok + 128],
                                ident[:], start=True, stop=True))
                        ev_t1 = r.tick("T")
                        r.wait("V", ev_t1)
                        r.v(lambda e: e.tensor_copy(
                            tmst[:], psv(bt1, 0, 0, [[0, 128], [1, 256]])))
                        ev_tm = r.tick("V")
                        r.wait("T", ev_tm)
                        for grp in range(4):
                            for hh in range(4):
                                h = grp * 4 + hh
                                r.t(lambda e, h=h, hh=hh: e.matmul(
                                    psv(bt2[hh], 0, 0, [[0, 16], [1, 128]]),
                                    tmst[:, h * 16: h * 16 + 16],
                                    ident[:], start=True, stop=True))
                            ev_t2 = r.tick("T")
                            r.wait("V", ev_t2)
                            bstep = (bt2[1] - bt2[0]) * 512
                            r.v(lambda e, grp=grp, dst0=dst0, tt2=tt2, bstep=bstep: e.tensor_copy(
                                bass.AP(qkh, dst0 + grp * 4 * 256 + tt2 * 128,
                                        [[8192, 16], [256, 4], [1, 128]]),
                                psv(bt2[0], 0, 0, [[0, 16], [bstep, 4], [1, 128]])))
                            ev_done = r.tick("V")
                            r.wait("T", ev_done)
                return ev_done

            ev_rst = restage(0, 0, (4, 5, 6, 7), (ev_qk,))
            dbg_cap("d_qkh", lambda: ap(qkh, 0, 0, [[0, 16], [1, 8192]]))
            ev_phi = build_phis(0, (4, 5, 6, 7), (ev_rs, ev_qk, ev_rst))
            dbg_cap("d_phiqa", lambda: ap(pq, 0, 0, [[0, 68], [1, 16 * 256]]))

            ev_as = {}
            ev_sqw = {}
            ev_sadd_z = ev_sadd
            ev_epi_done = None
            ev_szcast = None
            ev_pkm = {}

            def build_phiktm(cidx, buf, prev_mult_ev):
                first_evac = True
                ev_e = None
                for h in range(H):
                    half, h8 = divmod(h, 8)
                    r.t(lambda e, h8=h8, half=half, cidx=cidx: e.matmul(
                        psv(0, 0, 0, [[0, 64], [1, 136]]),
                        qk_fm[:, 1024 + half * 512 + cidx * 64:
                              1024 + half * 512 + cidx * 64 + 64],
                        rm_sb[:, (h8 * 2) * 136: (h8 * 2) * 136 + 136],
                        start=True, stop=True))
                    r.t(lambda e, h8=h8, half=half, cidx=cidx: e.matmul(
                        psv(3, 0, 180, [[0, 64], [1, 136]]),
                        qk_fm[:, 1024 + half * 512 + cidx * 64:
                              1024 + half * 512 + cidx * 64 + 64],
                        rm_sb[:, (h8 * 2 + 1) * 136: (h8 * 2 + 1) * 136 + 136],
                        start=True, stop=True))
                    ev12 = r.tick("T")
                    r.wait("S", ev12)
                    if first_evac:
                        r.wait("S", prev_mult_ev)
                        first_evac = False
                    r.s(lambda e, h=h: e.activation(
                        ap(pk1, 0, h * 136, [[0, 64], [1, 136]]),
                        psv(0, 0, 0, [[0, 64], [1, 136]]),
                        AF.Identity, bias=0.0, scale=1.0))
                    r.s(lambda e, h=h: e.activation(
                        ap(pk2, 0, h * 136, [[0, 64], [1, 136]]),
                        psv(3, 0, 180, [[0, 64], [1, 136]]),
                        AF.Identity, bias=0.0, scale=1.0))
                    ev_e = r.tick("S")
                    r.wait("T", ev_e)
                r.wait("V", ev_e, prev_mult_ev)
                r.v(lambda e, buf=buf: e.tensor_tensor(
                    phiKtm[buf][:], pk1[:], pk2[:], OP.mult))
                return r.tick("V")

            for c in range(NCH):
                gc = gc0 + c
                if c == 0:
                    r.wait("T", ev_phi, ev_sadd_z, prev_ffn_done_V)
                    ev_pkm[0] = build_phiktm(0, 0, ev_pkm.get(-1))
                if c == 4:
                    ev_rst2 = restage(1, 0, (1, 2, 3, 4),
                                      (ev_epi_done, ev_as[1], ev_sadd_z, ev_pkm[c]))
                    ev_phi = build_phis(1, (0, 1, 2, 3), (ev_rst2,))
                if c == 0 and gc0 == 0:
                    dbg_cap("d_phiktm", lambda: phiKtm[0][:])

                if gc > 0:
                    r.wait("V", ev_sadd_z)
                    r.v(lambda e: e.tensor_copy(SZbf[:], Ssb[:]))
                    ev_szcast = r.tick("V")
                    if gc == 1 and dbg and "d_szc" not in dbg_done and "szc" not in _DBG_SKIP:
                        r.v(lambda e: e.tensor_copy(ap(ep, 0, 0, [[0, 68], [1, 65]]),
                                                    ap(SZbf, 0, 0, [[0, 68], [1, 65]])))
                        r.v(lambda e: e.tensor_copy(ap(ep, 0, 65, [[0, 68], [1, 65]]),
                                                    ap(SZbf, 0, 1040, [[0, 68], [1, 65]])))
                        dbg_cap("d_szc", lambda: ap(ep, 0, 0, [[0, 68], [1, 130]]))

                for wave in range(2):
                    r.wait("T", ev_phi, ev_sqw.get(1 - wave), ev_pkm[c])
                    for hh in range(8):
                        h = wave * 8 + hh
                        r.t(lambda e, h=h, c=c, hh=hh: e.matmul(
                            psv(0, 0, hh * 64, [[0, 64], [1, 64]]),
                            ap(qkh, 0, 4096 + h * 256 + (c % 4) * 64, [[0, 16], [1, 64]]),
                            ap(qkh, 0, h * 256 + (c % 4) * 64, [[0, 16], [1, 64]]),
                            start=True, stop=True))
                    ev_pa = r.tick("T")
                    r.wait("S", ev_pa, ev_as.get(wave))
                    r.s(lambda e, wave=wave: e.activation(
                        epv(wave, 64), psv(0, 0, 0, [[0, 64], [1, 512]]),
                        AF.Square, bias=0.0, scale=1.0))
                    ev_sq = r.tick("S")
                    ev_sqw[wave] = ev_sq
                    r.wait("V", ev_sq, ev_mk)
                    r.v(lambda e, wave=wave: e.tensor_tensor(
                        ap(As_sb, 0, wave * 512, [[0, 64], [1, 512]]),
                        epv(wave, 64),
                        ap(maskrep, 0, wave * 512, [[0, 64], [1, 512]]),
                        OP.mult))
                    ev_as[wave] = r.tick("V")
                ev_as_all = ev_as[1]

                po_banks = [(1, 0, 7), (2, 7, 7), (3, 14, 2)]
                r.wait("T", ev_as_all, ev_szcast, ev_epi_done, ev_sadd_z)
                for bank, h0, nh in po_banks:
                    for hh in range(nh):
                        h = h0 + hh
                        first = True
                        if gc > 0:
                            for t in range(NTILE):
                                r.t(lambda e, bank=bank, hh=hh, h=h, t=t, c=c: e.matmul(
                                    psv(bank, 0, hh * 65, [[0, 64], [1, 65]]),
                                    ap(pq, 0, (t * H + h) * 256 + (c % 4) * 64, [[0, 68], [1, 64]]),
                                    ap(SZbf, 0, (t * H + h) * 65, [[0, 68], [1, 65]]),
                                    start=(t == 0), stop=False))
                            first = False
                        r.t(lambda e, bank=bank, hh=hh, h=h, c=c, first=first: e.matmul(
                            psv(bank, 0, hh * 65, [[0, 64], [1, 65]]),
                            ap(As_sb, 0, h * 64, [[0, 64], [1, 64]]),
                            bigv(0, V1C + c * 1040 + h * 65, [[0, 64], [1, 65]]),
                            start=first, stop=True))
                ev_po = r.tick("T")

                r.wait("V", ev_po)
                if c == 0 and gc0 == 0 and dbg and "d_po" not in dbg_done:
                    r.v(lambda e: e.tensor_copy(epv(2, 64, 512), psv(1, 0, 0, [[0, 64], [1, 512]])))
                    dbg_cap("d_po", lambda: epv(2, 64, 512))
                if c == 0 and gc0 == 0 and dbg and "d_ddr" not in dbg_done:
                    pass
                r.wait("S", ev_po)
                for bank, h0, nh in po_banks:
                    r.s(lambda e, bank=bank, h0=h0, nh=nh: e.activation(
                        ap(dd, 0, h0, [[0, 64], [1, nh]]),
                        psv(bank, 0, 64, [[0, 64], [65, nh]]),
                        AF.Identity, bias=EPS, scale=1.0))
                ev_dd = r.tick("S")
                if gc == gc0 + 1 and gc0 == 0 and dbg and "d_poc1" not in dbg_done and "poc1" not in _DBG_SKIP:
                    r.wait("V", ev_po)
                    r.v(lambda e: e.tensor_copy(epv(2, 64, 130), psv(1, 0, 0, [[0, 64], [1, 130]])))
                    dbg_cap("d_poc1", lambda: epv(2, 64, 130))
                if gc == gc0 + 1 and gc0 == 0 and dbg and "d_ddc1" not in dbg_done and "ddc1" not in _DBG_SKIP:
                    r.wait("V", ev_dd)
                    r.v(lambda e: e.tensor_copy(ap(ep, 0, 1536, [[0, 64], [1, 16]]), dd[:]))
                    dbg_cap("d_ddc1", lambda: ap(ep, 0, 1536, [[0, 64], [1, 16]]))
                r.wait("V", ev_dd)
                r.v(lambda e: e.reciprocal(ddr[:], dd[:]))
                r.v(lambda e: e.tensor_copy(stats[:, 9:10], stats[:, 0:1]))
                r.v(lambda e: e.tensor_copy(stats[:, 10:11], stats[:, 0:1]))
                if c == 0 and gc0 == 0 and dbg and "d_ddr" not in dbg_done:
                    dbg_cap("d_ddr", lambda: ddr[:])
                for bank, h0, nh in po_banks:
                    for hh in range(nh):
                        h = h0 + hh
                        r.v(lambda e, bank=bank, hh=hh, h=h, c=c: e.tensor_scalar(
                            bass.AP(big, OTM + c * 1024 + h * 64, [[33024, 64], [1, 64]]),
                            psv(bank, 0, hh * 65, [[0, 64], [1, 64]]),
                            ddr[:, h:h + 1], None, OP.mult))
                ev_epi_done = r.tick("V")
                if c == 0 and gc0 == 0:
                    dbg_cap("d_dd", lambda: ap(dd, 0, 0, [[0, 64], [1, 16]]))
                    dbg_cap("d_as", lambda: ap(As_sb, 0, 0, [[0, 64], [1, 1024]]))

                SD_BANKS = [(4, 0, 0, 7), (5, 7, 0, 7), (6, 14, 0, 7),
                            (7, 21, 0, 7), (3, 28, 140, 4)]
                r.wait("T", ev_pkm[c], ev_sadd_z)
                for bank, s0, coff, ns in SD_BANKS:
                    if bank == 3:
                        r.wait("T", ev_epi_done)
                    for si in range(ns):
                        s = s0 + si
                        t, h = divmod(s, 16)
                        r.t(lambda e, bank=bank, si=si, coff=coff, t=t, h=h, c=c: e.matmul(
                            psv(bank, 0, coff + si * 65, [[0, 68], [1, 65]]),
                            ap(phiKtm[c % 2], 0, h * 136 + t * 68, [[0, 64], [1, 68]]),
                            bigv(0, V1C + c * 1040 + h * 65, [[0, 64], [1, 65]]),
                            start=True, stop=True))
                ev_sd = r.tick("T")

                r.wait("V", ev_sd)
                for bank, s0, coff, ns in SD_BANKS:
                    r.v(lambda e, bank=bank, s0=s0, coff=coff, ns=ns: e.tensor_tensor(
                        bass.AP(Ssb, s0 * 65, [[2080, 68], [1, ns * 65]]),
                        psv(bank, 0, coff, [[0, 68], [1, ns * 65]]),
                        bass.AP(Ssb, s0 * 65, [[2080, 68], [1, ns * 65]]),
                        OP.add))
                ev_sadd_z = r.tick("V")

                if c + 1 < NCH:
                    r.wait("T", ev_sadd_z, ev_pkm[c])
                    ev_pkm[c + 1] = build_phiktm(c + 1, (c + 1) % 2, ev_pkm[c])
            ev_sadd = ev_sadd_z
            if gc0 == 0:
                dbg_cap("d_otm", lambda: bigv(0, OTM, [[0, 64], [1, 8 * 1024]]))
                dbg_cap("d_sz", lambda: ap(Ssb, 0, 0, [[0, 68], [1, 2080]]))

            r.wait("T", ev_epi_done, ev_sadd, ev_id)
            ev_ofm = None
            for k in range(8):
                for c in range(NCH):
                    r.t(lambda e, k=k, c=c: e.matmul(
                        psv(6, 0, c * 64, [[0, 128], [1, 64]]),
                        bass.AP(big, OTM + c * 1024 + k * 128, [[33024, 64], [1, 128]]),
                        ident[0:64, 0:64], start=True, stop=True))
                ev_tr = r.tick("T")
                r.wait("V", ev_tr)
                r.v(lambda e, k=k: e.tensor_copy(
                    o_fm_v(k * 512, 512),
                    psv(6, 0, 0, [[0, 128], [1, 512]])))
                ev_ofm = r.tick("V")
                r.wait("T", ev_ofm)

            ev_w1 = {0: load_w1(0, 0)}
            ev_w1_mm_done = {}
            ev_z0_by_bank = {}
            ev_hb_prev = None
            ev_y2_prev = None
            for fft in range(32):
                bank = fft % 2
                if fft + 1 < 32:
                    if fft >= 1:
                        r.wait("G", ev_w1_mm_done[fft - 1])
                    ev_w1[fft + 1] = load_w1((fft + 1) % 2, fft + 1)
                r.wait("T", ev_w1[fft], ev_z0_by_bank.get(bank), ev_b1s)
                for k in range(8):
                    r.t(lambda e, fft=fft, k=k, bank=bank: e.matmul(
                        psv(bank, 0, 0, [[0, 128], [1, 512]]),
                        w1_sb[fft % 2][:, k * 128: k * 128 + 128],
                        xn_fm[:, k * 512: (k + 1) * 512],
                        start=(k == 0), stop=(k == 7)))
                ev_h = r.tick("T")
                ev_w1_mm_done[fft] = ev_h
                r.wait("S", ev_h, ev_hb_prev, ev_y2_prev)
                r.s(lambda e, bank=bank, fft=fft: e.activation(
                    epv(0), psv(bank, 0, 0, [[0, 128], [1, 512]]),
                    AF.Identity, bias=b1s[:, fft:fft + 1], scale=256.0))
                ev_z0 = r.tick("S")
                ev_z0_by_bank[bank] = ev_z0
                r.wait("V", ev_z0, ev_y2_prev)
                r.v(lambda e: e.tensor_scalar(epv(1), epv(0), MAGIC,
                                              MAGIC - 1024.0, OP.add, OP.max))
                r.v(lambda e: e.tensor_scalar(epv(2), epv(1), MAGIC + 1024.0,
                                              -(MAGIC - 1024.0), OP.min, OP.add))
                ev_vp = r.tick("V")
                r.wait("S", ev_vp)
                r.s(lambda e: e.activation(epv(3), epv(2), AF.Identity,
                                           bias=0.0, scale=1.0 / 64.0))
                r.s(lambda e: e.activation(epi[:], epv(2), AF.Identity,
                                           bias=0.0, scale=1.0 / 64.0))
                ev_i0 = r.tick("S")
                r.wait("V", ev_i0)
                r.v(lambda e: e.tensor_tensor(epv(4), epi[:], epv(3), OP.is_gt))
                r.v(lambda e: e.tensor_tensor(epv(2), epi[:], epv(4), OP.subtract))
                r.v(lambda e: e.tensor_tensor(epv(3), epv(3), epv(2), OP.subtract))
                ev_it = r.tick("V")
                r.wait("S", ev_it)
                r.s(lambda e: e.activation(epv(4), epv(2), AF.Gelu,
                                           bias=-4.0, scale=0.25))
                r.s(lambda e: e.activation(epv(5), epv(2), AF.Gelu,
                                           bias=-3.75, scale=0.25))
                ev_g = r.tick("S")
                r.wait("V", ev_g)
                r.v(lambda e: e.tensor_tensor(epv(5), epv(5), epv(4), OP.subtract))
                r.v(lambda e: e.tensor_tensor(epv(5), epv(3), epv(5), OP.mult))
                r.v(lambda e: e.tensor_tensor(epv(4), epv(5), epv(4), OP.add))
                r.v(lambda e: e.tensor_scalar(epi[:], epv(1), MAGIC + 1024.0,
                                              None, OP.is_gt))
                r.v(lambda e: e.tensor_scalar(epv(2), epv(1), -MAGIC,
                                              1.0 / 256.0, OP.add, OP.mult))
                r.v(lambda e: e.select(epv(3), epi[:], epv(2), epv(4)))
                r.v(lambda e: e.scalar_tensor_tensor(epv(4), epv(0), -1024.5,
                                                     epv(3), OP.is_ge, OP.mult))
                ev_y2 = r.tick("V")
                ev_y2_prev = ev_y2
                r.wait("S", ev_y2)
                r.s(lambda e: e.activation(epv(5), epv(4), AF.Identity,
                                           bias=MAGIC + 1024.0, scale=256.0))
                r.s(lambda e, fft=fft: e.activation(
                    bass.AP(big, H_COLS + fft * 512, [[33024, 128], [1, 512]]),
                    epv(5), AF.Identity, bias=-49156.0, scale=1.0 / 256.0))
                ev_hb_prev = r.tick("S")
                if fft == 0 and gc0 == 0 and b == 0:
                    dbg_cap("d_h", lambda: epv(4))
            ev_h_all = ev_hb_prev

            ev_w2 = {(0, 0): load_w2(0, 0, 0)}
            r.wait("G", ev_sadd)
            ev_wo_dh = {0: load_wo(0)}
            ev_out = None
            ev_z4_prev = None
            prev_store_any = None
            ev_mm2_read = {}
            ev_mm2_sl_prev = None
            for dh in range(2):
                for sl in range(16):
                    nxt = (dh, sl + 1) if sl + 1 < 16 else ((1, 0) if dh == 0 else None)
                    if nxt is not None:
                        if (dh, sl) != (0, 0):
                            r.wait("G", ev_mm2_sl_prev)
                        ev_w2[nxt] = load_w2((sl + 1 + dh * 16) % 2, nxt[0], nxt[1])
                    r.wait("T", ev_w2[(dh, sl)],
                           ev_hb_prev if (dh == 0 and sl == 0) else None,
                           ev_mm2_read.get(dh - 1))
                    for ffcl in range(2):
                        ffc = sl * 2 + ffcl
                        for tt in range(NTT):
                            r.t(lambda e, dh=dh, sl=sl, ffcl=ffcl, ffc=ffc, tt=tt: e.matmul(
                                psv(2 + tt, 0, 0, [[0, 128], [1, 512]]),
                                bass.AP(big, H_COLS + ffc * 512 + tt * 128,
                                        [[33024, 128], [1, 128]]),
                                w2_sb[(sl + dh * 16) % 2][:, ffcl * 512: ffcl * 512 + 512],
                                start=(sl == 0 and ffcl == 0), stop=(sl == 15 and ffcl == 1)))
                    ev_mm2_sl_prev = r.tick("T")
                ev_mm2 = ev_mm2_sl_prev
                for tt in range(NTT):
                    row = tok0 + tt * 128
                    r.wait("Q", ev_y2_prev, ev_out)
                    evxr = r.q(lambda e, row=row: e.dma_start(
                        xr_v(), bass.AP(x_d.tensor, row * D, [[D, 128], [1, D]])), "qx")
                    r.wait("V", ev_mm2, ev_b2t, prev_store_any, ev_z4_prev)
                    r.v(lambda e, tt=tt, dh=dh: e.scalar_tensor_tensor(
                        osb[0][:], psv(2 + tt, 0, 0, [[0, 128], [1, 512]]),
                        256.0, b2t[:, dh * 512: dh * 512 + 512], OP.mult, OP.add))
                    ev_z3 = r.tick("V")
                    r.wait("S", ev_z3, ev_out)
                    r.s(lambda e: e.activation(u4_v(), osb[0][:], AF.Identity,
                                               bias=MAGIC, scale=1.0))
                    ev_z4_prev = r.tick("S")
                    r.s(lambda e: e.activation(q3_v(), u4_v(), AF.Identity,
                                               bias=-49152.0, scale=1.0 / 256.0))
                    ev_q3 = r.tick("S")
                    r.wait("T", ev_z3, ev_ofm, ev_wo_dh[dh])
                    for k in range(8):
                        r.t(lambda e, k=k, tt=tt: e.matmul(
                            psv(2 + tt, 0, 0, [[0, 128], [1, 512]]),
                            o_fm_v(k * 512 + tt * 128, 128),
                            ap(pq, 0, 8192 + k * 512, [[0, 128], [1, 512]]),
                            start=(k == 0), stop=(k == 7)))
                    ev_wom = r.tick("T")
                    if tt == NTT - 1 and dh == 0:
                        r.wait("G", ev_wom)
                        ev_wo_dh[1] = load_wo(1)
                    r.wait("V", ev_wom, evxr, ev_q3)
                    r.v(lambda e, dh=dh, tt=tt: e.tensor_tensor(
                        osb[1][:], xr_h(dh),
                        psv(2 + tt, 0, 0, [[0, 128], [1, 512]]), OP.add))
                    r.v(lambda e: e.tensor_tensor(osb[1][:], osb[1][:], q3_v(), OP.add))
                    ev_out = r.tick("V")
                    r.wait("Q", ev_out)
                    ev_st = r.q(lambda e, row=row, dh=dh: e.dma_start(
                        bass.AP(out_d.tensor, row * D + dh * 512, [[D, 128], [1, 512]]),
                        osb[1][:]), "qout")
                    prev_store_any = ev_st
                    last_out_ev = ev_st
                ev_mm2_read[dh] = ev_out
            prev_ffn_done_V = ev_out
            prev_ffn_done_S = ev_q3

    r.final_waits("Q")
    sems = {}
    for e in list("TVS") + sorted(r.streams):
        sems[e] = es.enter_context(nc.semaphore(f"sem{e}"))
    with es:
        r.replay(nc, sems)
    return nc


def kernel(L0_attn_hidden_states, L0_attn_Wq, L0_attn_Wk, L0_attn_Wv, L0_attn_Wo,
           L0_ffn_w1, L0_ffn_b1, L0_ffn_w2, L0_ffn_b2):
    x = np.ascontiguousarray(np.asarray(L0_attn_hidden_states, np.float32))
    consts = host_consts()
    shared = dict(
        wq=np.ascontiguousarray(L0_attn_Wq, np.float32),
        wk=np.ascontiguousarray(L0_attn_Wk, np.float32),
        wv=np.ascontiguousarray(L0_attn_Wv, np.float32),
        wo=np.ascontiguousarray(L0_attn_Wo, np.float32),
        w1=np.ascontiguousarray(L0_ffn_w1, np.float32),
        b1=np.ascontiguousarray(L0_ffn_b1, np.float32),
        w2=np.ascontiguousarray(L0_ffn_w2, np.float32),
        b2=np.ascontiguousarray(L0_ffn_b2, np.float32),
        **consts)
    in_maps = []
    for c in range(NCORES):
        m = dict(shared)
        m["x"] = np.ascontiguousarray(
            x[c * B_PER_CORE:(c + 1) * B_PER_CORE].reshape(B_PER_CORE * L, D))
        in_maps.append(m)
    nc = build_nc(dbg=_DEBUG)
    res = run_bass_kernel_spmd(nc, in_maps, core_ids=list(range(NCORES)))
    out = np.concatenate([r["out"].reshape(B_PER_CORE, L, D) for r in res.results],
                         axis=0)
    if _DEBUG:
        kernel._dbg = list(res.results)
    return out
